# version 06
# baseline (speedup 1.0000x reference)
import numpy as np
import concourse.bass as bass
import concourse.mybir as mybir
from concourse.bass_utils import run_bass_kernel_spmd

F32 = mybir.dt.float32
BF16 = mybir.dt.bfloat16
AF = mybir.ActivationFunctionType
ALU = mybir.AluOpType
AX = mybir.AxisListType

T = 4096
D = 1024
DI = 2048
DS = 16
NG = 16
KD = 8
TC = 256
NCH = T // TC
SLAB = 512
NSL = T // SLAB
NE = 8
HID = 2048
NHK = 16
EPS = 1e-5


class Sem:
    def __init__(self, nc, name):
        self.name = name
        self._cm = nc.semaphore(name)
        self.h = self._cm.__enter__()
        self.count = 0

    def inc(self, n=1):
        self.count += n
        return self.count

    def bump(self):
        self.count += 1
        return (self, 1)

    def ev(self):
        return (self, self.count)


class Emitter:
    ENGS = ("sync", "tensor", "scalar", "vector")

    def __init__(self):
        self.items = {e: [] for e in self.ENGS}
        self.hsem = {}

    def emit(self, eng, fn, reads=(), writes=(), waits=(), incs=()):
        ws = [w for w in waits if w is not None]
        for (s, v) in ws:
            assert v <= s.count, f"wait on future event {s.name}:{v}>{s.count}"
        evs_after = tuple((s, s.count) for (s, _a) in incs)
        self.items[eng].append((tuple(ws), fn, tuple(incs), evs_after,
                                frozenset(reads), frozenset(writes)))

    def replay(self, block):
        def mk(name):
            def body(eng):
                last_w = frozenset()
                last_r = frozenset()
                last_inst = [None]
                last_evs = [()]
                waited = {}
                hs = self.hsem.get(name)
                chk = name in ("scalar", "vector")
                for (waits, fn, incs, evs_after, reads, writes) in self.items[name]:
                    hazard = chk and (((reads | writes) & last_w)
                                      or (writes & last_r))
                    ws = list(waits)
                    if hazard and last_inst[0] is not None:
                        if last_evs[0]:
                            ws.append(last_evs[0][0])
                        else:
                            hs.count += 1
                            last_inst[0].then_inc(hs.h, 1)
                            ws.append((hs, hs.count))
                    for (s, v) in ws:
                        if waited.get(s.name, -1) >= v:
                            continue
                        eng.wait_ge(s.h, v)
                        waited[s.name] = v
                    inst = fn(eng)
                    for (s, v) in incs:
                        inst.then_inc(s.h, v)
                    last_inst[0] = inst
                    last_evs[0] = evs_after
                    last_w = writes
                    last_r = reads
            return body
        for n in self.ENGS:
            getattr(block, n)(mk(n))


def build_program():
    nc = bass.Bass("TRN2", target_bir_lowering=False, debug=False,
                   dynamic_dma_scratch_size=4096)

    x_dr = nc.dram_tensor("x", [T, D], F32, kind="ExternalInput").ap()
    xw_dr = nc.dram_tensor("x_proj_w", [DI, D], F32, kind="ExternalInput").ap()
    dtw_dr = nc.dram_tensor("dt_proj_w", [DI, D], F32, kind="ExternalInput").ap()
    dtb_dr = nc.dram_tensor("dt_proj_b", [DI], F32, kind="ExternalInput").ap()
    alog_dr = nc.dram_tensor("A_log", [DI, DS], F32, kind="ExternalInput").ap()
    dvec_dr = nc.dram_tensor("Dvec", [DI], F32, kind="ExternalInput").ap()
    bw_dr = nc.dram_tensor("B_proj_w", [DS, D], F32, kind="ExternalInput").ap()
    cw_dr = nc.dram_tensor("C_proj_w", [DS, D], F32, kind="ExternalInput").ap()
    convw_dr = nc.dram_tensor("conv_w", [DI, 4], F32, kind="ExternalInput").ap()
    convb_dr = nc.dram_tensor("conv_b", [DI], F32, kind="ExternalInput").ap()
    ow_dr = nc.dram_tensor("out_proj_w", [D, DI], F32, kind="ExternalInput").ap()
    wg_dr = nc.dram_tensor("Wg", [D, NE], F32, kind="ExternalInput").ap()
    w1_dr = nc.dram_tensor("W1", [NE, D, HID], F32, kind="ExternalInput").ap()
    w2_dr = nc.dram_tensor("W2", [NE, D, HID], F32, kind="ExternalInput").ap()
    w3_dr = nc.dram_tensor("W3", [NE, HID, D], F32, kind="ExternalInput").ap()
    out_dr = nc.dram_tensor("out", [T, D], F32, kind="ExternalOutput").ap()

    s_xpT = nc.dram_tensor("s_xpT", [NG, 128, KD * 128], BF16).ap()
    s_dtT = nc.dram_tensor("s_dtT", [NG, 128, KD * 128], BF16).ap()
    s_opT = nc.dram_tensor("s_opT", [NG, 128, D], BF16).ap()
    s_w1 = nc.dram_tensor("s_w1", [NE, NHK, 128, KD * 128], BF16).ap()
    s_w2 = nc.dram_tensor("s_w2", [NE, NHK, 128, KD * 128], BF16).ap()
    s_w3 = nc.dram_tensor("s_w3", [NE, NHK, 128, D], BF16).ap()
    s_r1n = nc.dram_tensor("s_r1n", [T, D], F32).ap()
    s_r1nT = nc.dram_tensor("s_r1nT", [KD, 128, T], BF16).ap()

    ident_np = np.eye(128, dtype=np.float32)
    basis_np = np.zeros((48, 32 * 128), np.float32)
    for s in range(DS):
        basis_np[s, s * 128:(s + 1) * 128] = 1.0
        basis_np[32 + s, (16 + s) * 128:(17 + s) * 128] = 1.0
    ident_dr = nc.inline_tensor(ident_np, "ident_c").ap()
    basis_dr = nc.inline_tensor(basis_np, "basis_c").ap()

    em = Emitter()
    G = {}

    d_cst = Sem(nc, "d_cst")
    d_pl = Sem(nc, "d_pl")
    d_ps = Sem(nc, "d_ps")
    d_ms = [Sem(nc, "d_ms0"), Sem(nc, "d_ms1")]
    d_x = [Sem(nc, "d_x0"), Sem(nc, "d_x1")]
    d_wa = [Sem(nc, "d_wa0"), Sem(nc, "d_wa1")]
    d_wo = [Sem(nc, "d_wo0"), Sem(nc, "d_wo1")]
    d_wb = [Sem(nc, "d_wb0"), Sem(nc, "d_wb1")]
    d_rtw = [Sem(nc, "d_rtw0"), Sem(nc, "d_rtw1")]
    d_rTw = [Sem(nc, "d_rTw0"), Sem(nc, "d_rTw1")]
    d_rr = [Sem(nc, "d_rr0"), Sem(nc, "d_rr1")]
    d_rt = [Sem(nc, "d_rt0"), Sem(nc, "d_rt1")]
    d_out = [Sem(nc, "d_out0"), Sem(nc, "d_out1")]
    s_pe = Sem(nc, "s_pe")
    s_act = Sem(nc, "s_act")
    s_dve = Sem(nc, "s_dve")
    em.hsem["scalar"] = Sem(nc, "h_act")
    em.hsem["vector"] = Sem(nc, "h_dve")
    em.hsem["tensor"] = Sem(nc, "h_pe")
    em.hsem["sync"] = Sem(nc, "h_sp")

    def dma(sem, out, in_, waits=(), key=None, slow=False):
        sem.inc(16)
        em.emit("sync",
                lambda e, o=out, i=in_, s=slow: e.dma_start(
                    out=o, in_=i, allow_slow_non_contiguous=s),
                waits=waits, incs=[(sem, 16)], writes={key or "dma"})
        return sem.ev()

    def gw(key):
        return G.get(key)

    ident_f = nc.alloc_sbuf_tensor("ident_f", [128, 128], F32)
    ident_b = nc.alloc_sbuf_tensor("ident_b", [128, 128], BF16)
    basis_b = nc.alloc_sbuf_tensor("basis_b", [48, 32 * 128], BF16)
    acst = nc.alloc_sbuf_tensor("acst", [128, NG * DS], F32)
    dhalf = nc.alloc_sbuf_tensor("dhalf", [128, NG], F32)
    convw_sb = nc.alloc_sbuf_tensor("convw_sb", [128, NG * 4], F32)
    convb_sb = nc.alloc_sbuf_tensor("convb_sb", [128, NG], F32)
    dtb_sb = nc.alloc_sbuf_tensor("dtb_sb", [128, NG], F32)
    wgr = nc.alloc_sbuf_tensor("wgr", [128, KD * NE], F32)
    bcT = nc.alloc_sbuf_tensor("bcT", [128, KD * 48], BF16)
    combine = nc.alloc_sbuf_tensor("combine", [128, (T // 128) * NE], F32)
    carry = nc.alloc_sbuf_tensor("carry", [128, NG * DS], F32)
    tails = nc.alloc_sbuf_tensor("tails", [128, NG * 4], F32)
    cst = nc.alloc_sbuf_tensor("cst", [128, 2], F32)

    pb = [nc.alloc_psum_tensor(f"pb{i}", [128, 512], F32) for i in range(8)]

    pg_f = nc.sbuf_tensor("ptmp_f", [128, 2 * 2048], F32)
    ptmp_f2 = pg_f.__enter__()
    pg_b = nc.sbuf_tensor("ptmp_b", [128, 2 * 2048], BF16)
    ptmp_b2 = pg_b.__enter__()

    class _Slot:
        def __init__(self, t, w):
            self.t, self.w, self.i = t, w, 0

        def ap(self):
            return self.t.ap()[:, self.i * self.w:(self.i + 1) * self.w]

        def flip(self):
            self.i ^= 1
    ptmp_f = _Slot(ptmp_f2, 2048)
    ptmp_b = _Slot(ptmp_b2, 2048)
    pg_s = nc.sbuf_tensor("pstage", [128, 2048], BF16)
    pstage = pg_s.__enter__()
    pg_bf = nc.sbuf_tensor("basis_f", [48, 32 * 128], F32)
    basis_f = pg_bf.__enter__()

    dma(d_cst, acst.ap().rearrange("p (g s) -> p g s", g=NG),
        alog_dr.rearrange("(g p) s -> p g s", p=128))
    _cst_transforms = []
    dma(d_cst, dhalf.ap(), dvec_dr.rearrange("(g p) -> p g", p=128), slow=True)
    dma(d_cst, convw_sb.ap().rearrange("p (g k) -> p g k", g=NG),
        convw_dr.rearrange("(g p) k -> p g k", p=128))
    dma(d_cst, convb_sb.ap(), convb_dr.rearrange("(g p) -> p g", p=128), slow=True)
    dma(d_cst, dtb_sb.ap(), dtb_dr.rearrange("(g p) -> p g", p=128), slow=True)
    dma(d_cst, wgr.ap().rearrange("p (k e) -> p k e", k=KD),
        wg_dr.rearrange("(k p) e -> p k e", p=128))
    dma(d_cst, ident_f.ap(), ident_dr)
    dma(d_cst, basis_f.ap(), basis_dr)
    evb = dma(d_cst, ptmp_f.ap()[0:16, 0:1024], bw_dr)
    evc = dma(d_cst, ptmp_f.ap()[16:32, 0:1024], cw_dr)
    ev_cstall = d_cst.ev()
    em.emit("scalar", lambda e: e.activation(out=acst.ap(), in_=acst.ap(), func=AF.Exp),
            reads={"acst"}, writes={"acst"}, waits=[ev_cstall],
            incs=[s_act.bump()])
    em.emit("vector", lambda e: e.tensor_scalar(
        out=acst.ap(), in0=acst.ap(), scalar1=-1.0, scalar2=None, op0=ALU.mult),
        reads={"acst"}, writes={"acst"}, waits=[s_act.ev()])
    em.emit("vector", lambda e: e.tensor_scalar(
        out=dhalf.ap(), in0=dhalf.ap(), scalar1=0.5, scalar2=None, op0=ALU.mult),
        reads={"dhalf"}, writes={"dhalf"}, waits=[ev_cstall])
    em.emit("scalar", lambda e: e.copy(out=ident_b.ap(), in_=ident_f.ap()),
            reads={"ident_f"}, writes={"ident_b"}, waits=[ev_cstall],
            incs=[s_act.bump()])
    ev_identb = s_act.ev()
    em.emit("scalar", lambda e: e.copy(out=basis_b.ap(), in_=basis_f.ap()),
            reads={"basis_f"}, writes={"basis_b"}, waits=[ev_cstall])
    em.emit("vector", lambda e: e.memset(cst.ap()[:, 0:1], EPS), writes={"cst"})
    em.emit("vector", lambda e: e.memset(cst.ap()[:, 1:2], -0.5), writes={"cst"})
    em.emit("vector", lambda e: e.memset(carry.ap(), 0.0), writes={"carry"})
    em.emit("vector", lambda e: e.memset(tails.ap(), 0.0), writes={"tails"})
    em.emit("vector", lambda e: e.memset(bcT.ap(), 0.0), writes={"bcT"},
            incs=[s_dve.bump()])
    ev_bcz = s_dve.ev()

    em.emit("scalar", lambda e: e.copy(out=ptmp_b.ap()[0:32, 0:1024],
                                       in_=ptmp_f.ap()[0:32, 0:1024]),
            reads={"ptmp_f"}, writes={"ptmp_b"}, waits=[ev_cstall],
            incs=[s_act.bump()])
    ev_cast = s_act.ev()
    G["ptmp_f"] = ev_cast
    for k in range(KD):
        em.emit("tensor", lambda e, k=k: e.transpose(
            out=pb[7].ap().bitcast(BF16)[:, 0:32],
            in_=ptmp_b.ap()[0:32, k * 128:(k + 1) * 128],
            identity=ident_b.ap()[0:32, 0:32]),
            reads={"ptmp_b", "ident_b"}, writes={"pb7"},
            waits=[ev_cast, ev_identb, gw("pb7")],
            incs=[s_pe.bump()])
        ev_t = s_pe.ev()
        em.emit("scalar", lambda e, k=k: e.copy(
            out=bcT.ap()[:, k * 48:k * 48 + 16],
            in_=pb[7].ap().bitcast(BF16)[:, 0:16]),
            reads={"pb7"}, writes={"bcT"}, waits=[ev_t, ev_bcz])
        em.emit("scalar", lambda e, k=k: e.copy(
            out=bcT.ap()[:, k * 48 + 32:k * 48 + 48],
            in_=pb[7].ap().bitcast(BF16)[:, 16:32]),
            reads={"pb7"}, writes={"bcT"},
            incs=[s_act.bump()])
        G["pb7"] = s_act.ev()

    def prep_projT(src_dr, dst_dr):
        for g in range(NG):
            ev_l = dma(d_pl, ptmp_f.ap()[:, 0:1024],
                       src_dr[g * 128:(g + 1) * 128, :], waits=[gw("ptmp_f")])
            em.emit("scalar", lambda e: e.copy(out=ptmp_b.ap()[:, 0:1024],
                                               in_=ptmp_f.ap()[:, 0:1024]),
                    reads={"ptmp_f"}, writes={"ptmp_b"},
                    waits=[ev_l, gw("ptmp_b")],
                    incs=[s_act.bump()])
            G["ptmp_f"] = s_act.ev()
            ev_c = s_act.ev()
            for k in range(KD):
                em.emit("tensor", lambda e, k=k: e.transpose(
                    out=pb[7].ap().bitcast(BF16)[:, (k % 8) * 128:(k % 8) * 128 + 128],
                    in_=ptmp_b.ap()[:, k * 128:(k + 1) * 128],
                    identity=ident_b.ap()),
                    reads={"ptmp_b"}, writes={"pb7"},
                    waits=[ev_c, gw("pb7")],
                    incs=[s_pe.bump()])
                ev_t = s_pe.ev()
                em.emit("scalar", lambda e, k=k: e.copy(
                    out=pstage.ap()[:, k * 128:(k + 1) * 128],
                    in_=pb[7].ap().bitcast(BF16)[:, (k % 8) * 128:(k % 8) * 128 + 128]),
                    reads={"pb7"}, writes={"pstage"},
                    waits=[ev_t, gw("pstage")],
                    incs=[s_act.bump()])
                G["pb7"] = s_act.ev()
            G["ptmp_b"] = s_act.ev()
            ev_s = s_act.ev()
            G["pstage"] = dma(d_ps, dst_dr[g], pstage.ap()[:, 0:1024],
                              waits=[ev_s])

    prep_projT(xw_dr, s_xpT)
    prep_projT(dtw_dr, s_dtT)

    for k in range(KD):
        ev_l = dma(d_pl, ptmp_f.ap(), ow_dr[k * 128:(k + 1) * 128, :],
                   waits=[gw("ptmp_f")])
        em.emit("scalar", lambda e: e.copy(out=ptmp_b.ap(), in_=ptmp_f.ap()),
                reads={"ptmp_f"}, writes={"ptmp_b"},
                waits=[ev_l, gw("ptmp_b")],
                incs=[s_act.bump()])
        G["ptmp_f"] = s_act.ev()
        ev_c = s_act.ev()
        for g in range(NG):
            em.emit("tensor", lambda e, g=g: e.transpose(
                out=pb[7].ap().bitcast(BF16)[:, (g % 8) * 128:(g % 8) * 128 + 128],
                in_=ptmp_b.ap()[:, g * 128:(g + 1) * 128],
                identity=ident_b.ap()),
                reads={"ptmp_b"}, writes={"pb7"},
                waits=[ev_c, gw("pb7")],
                incs=[s_pe.bump()])
            ev_t = s_pe.ev()
            em.emit("scalar", lambda e, g=g: e.copy(
                out=pstage.ap()[:, g * 128:(g + 1) * 128],
                in_=pb[7].ap().bitcast(BF16)[:, (g % 8) * 128:(g % 8) * 128 + 128]),
                reads={"pb7"}, writes={"pstage"},
                waits=[ev_t, gw("pstage")],
                incs=[s_act.bump()])
            G["pb7"] = s_act.ev()
            ev_s = s_act.ev()
            G["pstage"] = dma(d_ps, s_opT[g][:, k * 128:(k + 1) * 128],
                               pstage.ap()[:, g * 128:(g + 1) * 128],
                               waits=[ev_s])
        G["ptmp_b"] = s_act.ev()

    def moe_prep_unit(u):
        ptmp_f.flip()
        ptmp_b.flip()
        fi, bi = ptmp_f.i, ptmp_b.i
        if u < 128:
            wsrc, wdst = (w1_dr, s_w1) if u < 64 else (w2_dr, s_w2)
            e_i, k = divmod(u % 64, KD)
            ev_l = dma(d_pl, ptmp_f.ap(), wsrc[e_i, k * 128:(k + 1) * 128, :],
                       waits=[gw(("ptmp_f", fi))])
            em.emit("scalar", lambda e, a=ptmp_b.ap(), b=ptmp_f.ap(): e.copy(out=a, in_=b),
                    reads={f"ptmp_f{fi}"}, writes={f"ptmp_b{bi}"},
                    waits=[ev_l, gw(("ptmp_b", bi))],
                    incs=[s_act.bump()])
            G[("ptmp_f", fi)] = s_act.ev()
            ev_c = s_act.ev()
            dst = wdst[e_i].rearrange("hg p c -> p hg c")[:, :, k * 128:(k + 1) * 128]
            srcap = ptmp_b.ap().rearrange("p (hg c) -> p hg c", hg=NHK)
            G[("ptmp_b", bi)] = dma(d_ms[bi], dst, srcap, waits=[ev_c])
        else:
            e_i, hk = divmod(u - 128, NHK)
            ev_l = dma(d_pl, ptmp_f.ap()[:, 0:1024],
                       w3_dr[e_i, hk * 128:(hk + 1) * 128, :],
                       waits=[gw(("ptmp_f", fi))])
            em.emit("scalar", lambda e, a=ptmp_b.ap()[:, 0:1024],
                    b=ptmp_f.ap()[:, 0:1024]: e.copy(out=a, in_=b),
                    reads={f"ptmp_f{fi}"}, writes={f"ptmp_b{bi}"},
                    waits=[ev_l, gw(("ptmp_b", bi))],
                    incs=[s_act.bump()])
            G[("ptmp_f", fi)] = s_act.ev()
            ev_c = s_act.ev()
            G[("ptmp_b", bi)] = dma(d_ms[bi], s_w3[e_i, hk], ptmp_b.ap()[:, 0:1024],
                                    waits=[ev_c])

    prep_pe = s_pe.ev()
    prep_act = s_act.ev()
    prep_dma = d_ps.ev()
    prep_dma2 = d_pl.ev()
    G[("ptmp_f", 0)] = G.get("ptmp_f")
    G[("ptmp_b", 0)] = G.get("ptmp_b")

    pg_bf.__exit__(None, None, None)

    ga = []

    def A(name, shape, dt):
        g_ = nc.sbuf_tensor(name, shape, dt)
        t = g_.__enter__()
        ga.append(g_)
        return t

    x_tok = A("x_tok", [128, 2 * 2048], F32)
    xbf = A("xbf", [128, 2048], BF16)
    xT = A("xT", [128, KD * TC], BF16)
    xpw_t = A("xpw", [128, 2 * 1024], BF16)
    xpw = lambda i: xpw_t.ap()[:, i * 1024:(i + 1) * 1024]
    dtw_t = A("dtw_sb", [128, 2 * 1024], BF16)
    dtw_sb = lambda i: dtw_t.ap()[:, i * 1024:(i + 1) * 1024]
    opw_t = A("opw", [128, 2 * 1024], BF16)
    opw = lambda i: opw_t.ap()[:, i * 1024:(i + 1) * 1024]
    xi_t = A("xi_rot", [128, 2 * 264], F32)
    xi_rot = lambda i: xi_t.ap()[:, i * 264:(i + 1) * 264]
    cacc_t = A("cacc", [128, 2 * TC], F32)
    cacc = lambda i: cacc_t.ap()[:, i * TC:(i + 1) * TC]
    sgm = A("sgm", [128, TC], F32)
    sg0 = A("sg0", [128, TC], F32)
    sg1 = A("sg1", [128, TC], F32)
    sg2 = A("sg2", [128, TC], F32)
    sp_rot = A("sp_rot", [128, TC], F32)
    delta = A("delta", [128, NG * TC], F32)
    xc = A("xc", [128, NG * TC], F32)
    dtx = A("dtx", [128, NG * TC], BF16)
    bc_f = A("bc_f", [48, TC], BF16)
    bb = A("bb", [128, DS * TC], BF16)
    cb = A("cb", [128, DS * TC], BF16)
    a_t = A("a_buf", [128, 2 * 8 * TC], F32)
    a_buf = lambda i: a_t.ap()[:, i * 8 * TC:(i + 1) * 8 * TC]
    u_buf = A("u_buf", [128, 8 * TC], BF16)
    h_buf = A("h_buf", [128, 8 * TC], BF16)
    hc_st = A("hc_st", [128, DS * TC], BF16)
    treeF = A("treeF", [128, 2 * TC], F32)
    ytmp = A("ytmp", [128, TC], F32)
    bndt = A("bndt", [128, 16], F32)
    yb = A("yb", [128, NG * TC], BF16)
    r1_t = A("r1_rot", [128, 2 * 1024], F32)
    r1_rot = lambda i: r1_t.ap()[:, i * 1024:(i + 1) * 1024]
    sq_rot = A("sq_rot", [128, 1024], F32)
    r1n_t = A("r1n_rot", [128, 2 * 1024], F32)
    r1n_rot = lambda i: r1n_t.ap()[:, i * 1024:(i + 1) * 1024]
    r1nT_t = A("r1nT_rot", [128, 2 * 1024], BF16)
    r1nT_rot = lambda i: r1nT_t.ap()[:, i * 1024:(i + 1) * 1024]
    gate_t = A("gate_rot", [128, 2 * 1024], F32)
    gate_rot = lambda i: gate_t.ap()[:, i * 1024:(i + 1) * 1024]
    lgs = A("lgs", [128, 64], F32)
    ln_s = A("ln_s", [128, 16], F32)

    def mm(out_ap, lhsT, rhs, start, stop, waits=(), incs=(), reads=(), writes=()):
        em.emit("tensor", lambda e: e.matmul(out=out_ap, lhsT=lhsT, rhs=rhs,
                                             start=start, stop=stop),
                reads=set(reads), writes=set(writes), waits=waits, incs=incs)

    x3 = x_dr.rearrange("(c q p) d -> c q p d", q=2, p=128)

    for c in range(NCH):
        for u in range(c * 16, (c + 1) * 16):
            moe_prep_unit(u)
        par = c % 2
        for q in range(2):
            dma(d_x[par],
                x_tok.ap()[:, (par * 2 + q) * 1024:(par * 2 + q + 1) * 1024],
                x3[c, q],
                waits=[gw(("x_tok", par)),
                       prep_dma if (c == 0 and q == 0) else None,
                       prep_dma2 if (c == 0 and q == 0) else None])
        ev_x = [d_x[par].ev(), d_x[par].ev()]
        for q in range(2):
            em.emit("scalar", lambda e, q=q, par=par: e.copy(
                out=xbf.ap()[:, q * 1024:(q + 1) * 1024],
                in_=x_tok.ap()[:, (par * 2 + q) * 1024:(par * 2 + q + 1) * 1024]),
                reads={"x_tok"}, writes={"xbf"},
                waits=[ev_x[q], prep_act if c == 0 else None, gw("xbf")],
                incs=[s_act.bump()])
        ev_xbf = s_act.ev()
        for q in range(2):
            bk = pb[2 + q].ap().bitcast(BF16)
            for kd in range(KD):
                em.emit("tensor", lambda e, q=q, kd=kd, bk=bk: e.transpose(
                    out=bk[:, kd * 128:(kd + 1) * 128],
                    in_=xbf.ap()[:, q * 1024 + kd * 128:q * 1024 + (kd + 1) * 128],
                    identity=ident_b.ap()),
                    reads={"xbf"}, writes={f"pb{2+q}"},
                    waits=[ev_xbf, prep_pe if c == 0 else None,
                           gw(f"pb{2+q}")] if kd == 0 else [],
                    incs=[s_pe.bump()] if kd == KD - 1 else ())
            ev_tr = s_pe.ev()
            dst = xT.ap().rearrange("p (kd t) -> p kd t", kd=KD)[:, :, q * 128:(q + 1) * 128]
            em.emit("scalar", lambda e, q=q, bk=bk, dst=dst: e.copy(
                out=dst, in_=bk.rearrange("p (kd t) -> p kd t", kd=KD)),
                reads={f"pb{2+q}"}, writes={"xT"},
                waits=[ev_tr, gw("xT") if q == 0 else None],
                incs=[s_act.bump()])
            G[f"pb{2+q}"] = s_act.ev()
        ev_xT = s_act.ev()
        G["xbf"] = ev_tr

        for g in range(NG):
            gp = g % 2
            bxi = [0, 2][gp]
            bdt = [1, 3][gp]
            dma(d_wa[gp], xpw(gp), s_xpT[g], waits=[gw(("xpw", gp))])
            dma(d_wa[gp], dtw_sb(gp), s_dtT[g], waits=[gw(("dtw", gp))])
            ev_wx = ev_wd = d_wa[gp].ev()
            for k in range(KD):
                mm(pb[bxi].ap()[:, 0:TC],
                   xpw(gp)[:, k * 128:(k + 1) * 128],
                   xT.ap()[:, k * TC:(k + 1) * TC],
                   start=(k == 0), stop=(k == KD - 1),
                   waits=[ev_wx, ev_xT, gw(f"pb{bxi}")] if k == 0 else (),
                   incs=[s_pe.bump()] if k == KD - 1 else (),
                   reads={"xpw", "xT"}, writes={f"pb{bxi}"})
            ev_xi = s_pe.ev()
            G[("xpw", gp)] = ev_xi
            for k in range(KD):
                mm(pb[bdt].ap()[:, 0:TC],
                   dtw_sb(gp)[:, k * 128:(k + 1) * 128],
                   xT.ap()[:, k * TC:(k + 1) * TC],
                   start=(k == 0), stop=(k == KD - 1),
                   waits=[ev_wd, gw(f"pb{bdt}")] if k == 0 else (),
                   incs=[s_pe.bump()] if k == KD - 1 else (),
                   reads={"dtw_sb", "xT"}, writes={f"pb{bdt}"})
            ev_dt = s_pe.ev()
            G[("dtw", gp)] = ev_dt
            em.emit("scalar", lambda e, gp=gp, bxi=bxi: e.copy(
                out=xi_rot(gp)[:, 3:3 + TC], in_=pb[bxi].ap()[:, 0:TC]),
                reads={f"pb{bxi}"}, writes={"xi_rot"},
                waits=[ev_xi, gw(("xi", gp))])
            em.emit("scalar", lambda e, gp=gp, g=g: e.copy(
                out=xi_rot(gp)[:, 0:3], in_=tails.ap()[:, g * 4:g * 4 + 3]),
                reads={"tails"}, writes={"xi_rot"})
            em.emit("scalar", lambda e, gp=gp, g=g, bxi=bxi: e.copy(
                out=tails.ap()[:, g * 4:g * 4 + 3], in_=pb[bxi].ap()[:, TC - 3:TC]),
                reads={f"pb{bxi}"}, writes={"tails"},
                incs=[s_act.bump()])
            ev_xiev = s_act.ev()
            G[f"pb{bxi}"] = ev_xiev
            em.emit("scalar", lambda e, g=g, bdt=bdt: e.activation(
                out=sp_rot.ap(), in_=pb[bdt].ap()[:, 0:TC], func=AF.Exp,
                bias=dtb_sb.ap()[:, g:g + 1]),
                reads={f"pb{bdt}"}, writes={"sp_rot"},
                waits=[ev_dt, gw(("delta", g))])
            em.emit("scalar", lambda e, g=g: e.activation(
                out=delta.ap()[:, g * TC:(g + 1) * TC], in_=sp_rot.ap(),
                func=AF.Ln, bias=1.0),
                reads={"sp_rot"}, writes={"delta"},
                incs=[s_act.bump()])
            ev_delta = s_act.ev()
            G[f"pb{bdt}"] = ev_delta
            em.emit("vector", lambda e, gp=gp, g=g: e.tensor_scalar(
                out=cacc(gp), in0=xi_rot(gp)[:, 0:TC],
                scalar1=convw_sb.ap()[:, g * 4:g * 4 + 1],
                scalar2=convb_sb.ap()[:, g:g + 1], op0=ALU.mult, op1=ALU.add),
                reads={"xi_rot"}, writes={"cacc"},
                waits=[ev_xiev, gw(("cacc", gp))])
            for k in range(1, 4):
                em.emit("vector", lambda e, gp=gp, g=g, k=k: e.scalar_tensor_tensor(
                    out=cacc(gp), in0=xi_rot(gp)[:, k:k + TC],
                    scalar=convw_sb.ap()[:, g * 4 + k:g * 4 + k + 1],
                    in1=cacc(gp), op0=ALU.mult, op1=ALU.add),
                    reads={"xi_rot", "cacc"}, writes={"cacc"},
                    incs=[s_dve.bump()] if k == 3 else ())
            ev_conv = s_dve.ev()
            G[("xi", gp)] = ev_conv
            em.emit("vector", lambda e, gp=gp: e.tensor_scalar(
                out=sgm.ap(), in0=cacc(gp), scalar1=-43.0, scalar2=None,
                op0=ALU.max),
                reads={"cacc"}, writes={"sgm"},
                waits=[gw("sg")],
                incs=[s_dve.bump()])
            ev_sgm = s_dve.ev()
            em.emit("scalar", lambda e: e.activation(
                out=sg0.ap(), in_=sgm.ap(), func=AF.Exp, scale=-1.0),
                reads={"sgm"}, writes={"sg0"},
                waits=[ev_sgm])
            em.emit("scalar", lambda e: e.activation(
                out=sg1.ap(), in_=sg0.ap(), func=AF.Ln, bias=1.0),
                reads={"sg0"}, writes={"sg1"})
            em.emit("scalar", lambda e: e.activation(
                out=sg2.ap(), in_=sg1.ap(), func=AF.Exp, scale=-1.0),
                reads={"sg1"}, writes={"sg2"},
                incs=[s_act.bump()])
            ev_sg = s_act.ev()
            em.emit("vector", lambda e, gp=gp, g=g: e.scalar_tensor_tensor(
                out=xc.ap()[:, g * TC:(g + 1) * TC], in0=sg2.ap(), scalar=2.0,
                in1=cacc(gp), op0=ALU.mult, op1=ALU.mult),
                reads={"sg2", "cacc"}, writes={"xc"}, waits=[ev_sg])
            em.emit("vector", lambda e, g=g: e.tensor_mul(
                out=dtx.ap()[:, g * TC:(g + 1) * TC],
                in0=delta.ap()[:, g * TC:(g + 1) * TC],
                in1=xc.ap()[:, g * TC:(g + 1) * TC]),
                reads={"delta", "xc"}, writes={"dtx"}, waits=[ev_delta],
                incs=[s_dve.bump()])
            G["sg"] = s_dve.ev()
            G[("cacc", gp)] = s_dve.ev()
            G[("delta", g)] = s_dve.ev()
        ev_dtx_all = s_dve.ev()

        for k in range(KD):
            mm(pb[0].ap()[0:48, 0:TC],
               bcT.ap()[:, k * 48:(k + 1) * 48],
               xT.ap()[:, k * TC:(k + 1) * TC],
               start=(k == 0), stop=(k == KD - 1),
               waits=[gw("pb0")] if k == 0 else (),
               incs=[s_pe.bump()] if k == KD - 1 else (),
               reads={"bcT", "xT"}, writes={"pb0"})
        ev_bc = s_pe.ev()
        G["xT"] = ev_bc
        em.emit("scalar", lambda e: e.copy(out=bc_f.ap(), in_=pb[0].ap()[0:48, 0:TC]),
                reads={"pb0"}, writes={"bc_f"},
                waits=[ev_bc, gw("bc_f")],
                incs=[s_act.bump()])
        ev_bcf = s_act.ev()
        G["pb0"] = ev_bcf
        for j in range(32):
            bsel = 2 + (j % 2)
            mm(pb[bsel].ap()[:, 0:TC],
               basis_b.ap()[:, j * 128:(j + 1) * 128],
               bc_f.ap(),
               start=True, stop=True,
               waits=[ev_bcf, gw(f"pb{bsel}"),
                      gw("bbcb") if j < 2 else None],
               incs=[s_pe.bump()],
               reads={"basis_b", "bc_f"}, writes={f"pb{bsel}"})
            ev_bmm = s_pe.ev()
            if j < 16:
                em.emit("scalar", lambda e, j=j, bsel=bsel: e.activation(
                    out=bb.ap()[:, j * TC:(j + 1) * TC],
                    in_=pb[bsel].ap()[:, 0:TC], func=AF.Copy, scale=0.5),
                    reads={f"pb{bsel}"}, writes={"bb"}, waits=[ev_bmm],
                    incs=[s_act.bump()])
            else:
                em.emit("scalar", lambda e, j=j, bsel=bsel: e.copy(
                    out=cb.ap()[:, (j - 16) * TC:(j - 15) * TC],
                    in_=pb[bsel].ap()[:, 0:TC]),
                    reads={f"pb{bsel}"}, writes={"cb"}, waits=[ev_bmm],
                    incs=[s_act.bump()])
            G[f"pb{bsel}"] = s_act.ev()
        ev_bbcb = s_act.ev()
        G["bc_f"] = ev_bbcb

        for g in range(NG):
            for sh in range(2):
                ap_ = g % 2
                for si in range(8):
                    sgl = sh * 8 + si
                    em.emit("scalar", lambda e, g=g, si=si, sgl=sgl, ap_=ap_: e.activation(
                        out=a_buf(ap_)[:, si * TC:(si + 1) * TC],
                        in_=delta.ap()[:, g * TC:(g + 1) * TC], func=AF.Exp,
                        scale=acst.ap()[:, g * DS + sgl:g * DS + sgl + 1]),
                        reads={"delta"}, writes={f"a{ap_}"},
                        waits=[ev_dtx_all, gw(("a", ap_))] if si == 0 else (),
                        incs=[s_act.bump()] if si == 7 else ())
                ev_a = s_act.ev()
                dtx_b3 = dtx.ap()[:, g * TC:(g + 1) * TC].unsqueeze(1).broadcast_to([128, 8, TC])
                bb_3 = bb.ap().rearrange("p (s t) -> p s t", s=DS)[:, sh * 8:(sh + 1) * 8, :]
                u_3 = u_buf.ap().rearrange("p (s t) -> p s t", s=8)
                em.emit("vector", lambda e, dtx_b3=dtx_b3, bb_3=bb_3, u_3=u_3: e.tensor_tensor(
                    out=u_3, in0=dtx_b3, in1=bb_3, op=ALU.mult),
                    reads={"dtx", "bb"}, writes={"u_buf"},
                    waits=[ev_bbcb, ev_a])
                a_bnd = a_buf(ap_).rearrange("p (s t) -> p s t", s=8)[:, :, 0]
                u_bnd = u_buf.ap().rearrange("p (s t) -> p s t", s=8)[:, :, 0]
                cr = carry.ap()[:, g * DS + sh * 8:g * DS + sh * 8 + 8]
                em.emit("vector", lambda e, a_bnd=a_bnd, cr=cr: e.tensor_mul(
                    out=bndt.ap()[:, 0:8], in0=a_bnd, in1=cr),
                    reads={f"a{ap_}", "carry"}, writes={"bndt"})
                em.emit("vector", lambda e, u_bnd=u_bnd: e.tensor_add(
                    out=u_bnd, in0=u_bnd, in1=bndt.ap()[:, 0:8]),
                    reads={"u_buf", "bndt"}, writes={"u_buf"})
                em.emit("vector", lambda e, a_bnd=a_bnd: e.memset(a_bnd, 0.0),
                        writes={f"a{ap_}"})
                em.emit("vector", lambda e, ap_=ap_: e.tensor_tensor_scan(
                    out=h_buf.ap(), data0=a_buf(ap_), data1=u_buf.ap(),
                    initial=0.0, op0=ALU.mult, op1=ALU.add),
                    reads={f"a{ap_}", "u_buf"}, writes={"h_buf"})
                h_last = h_buf.ap().rearrange("p (s t) -> p s t", s=8)[:, :, TC - 1]
                em.emit("vector", lambda e, h_last=h_last, cr=cr: e.tensor_copy(
                    out=cr, in_=h_last),
                    reads={"h_buf"}, writes={"carry"})
                cb_sl = cb.ap()[:, sh * 8 * TC:(sh + 1) * 8 * TC]
                em.emit("vector", lambda e, sh=sh, cb_sl=cb_sl: e.tensor_mul(
                    out=hc_st.ap()[:, sh * 8 * TC:(sh + 1) * 8 * TC],
                    in0=h_buf.ap(), in1=cb_sl),
                    reads={"h_buf", "cb"}, writes={"hc_st"},
                    incs=[s_dve.bump()])
                G[("a", ap_)] = s_dve.ev()
            HT = DS * TC
            em.emit("vector", lambda e: e.tensor_add(
                out=hc_st.ap()[:, 0:HT // 2], in0=hc_st.ap()[:, 0:HT // 2],
                in1=hc_st.ap()[:, HT // 2:HT]),
                reads={"hc_st"}, writes={"hc_st"})
            em.emit("vector", lambda e: e.tensor_add(
                out=hc_st.ap()[:, 0:HT // 4], in0=hc_st.ap()[:, 0:HT // 4],
                in1=hc_st.ap()[:, HT // 4:HT // 2]),
                reads={"hc_st"}, writes={"hc_st"})
            em.emit("vector", lambda e: e.tensor_add(
                out=treeF.ap(), in0=hc_st.ap()[:, 0:2 * TC],
                in1=hc_st.ap()[:, 2 * TC:4 * TC]),
                reads={"hc_st"}, writes={"treeF"})
            em.emit("vector", lambda e: e.tensor_add(
                out=ytmp.ap(), in0=treeF.ap()[:, 0:TC], in1=treeF.ap()[:, TC:2 * TC]),
                reads={"treeF"}, writes={"ytmp"})
            em.emit("vector", lambda e, g=g: e.scalar_tensor_tensor(
                out=yb.ap()[:, g * TC:(g + 1) * TC],
                in0=xc.ap()[:, g * TC:(g + 1) * TC],
                scalar=dhalf.ap()[:, g:g + 1], in1=ytmp.ap(),
                op0=ALU.mult, op1=ALU.mult if False else ALU.add),
                reads={"xc", "ytmp"}, writes={"yb"},
                waits=[gw("yb")] if g == 0 else (),
                incs=[s_dve.bump()])
        ev_yb = s_dve.ev()

        for g in range(NG):
            gp = g % 2
            ev_wo = dma(d_wo[gp], opw(gp), s_opT[g], waits=[gw(("opw", gp))])
            for q in range(2):
                for dmh in range(2):
                    bi = 4 + q * 2 + dmh
                    last = (q == 1 and dmh == 1)
                    mm(pb[bi].ap(),
                       yb.ap()[:, g * TC + q * 128:g * TC + q * 128 + 128],
                       opw(gp)[:, dmh * 512:(dmh + 1) * 512],
                       start=(g == 0), stop=(g == NG - 1),
                       waits=[ev_yb, ev_wo, gw(f"pb{bi}"),
                              prep_pe if c == 0 else None] if g == 0 else [ev_wo],
                       incs=[s_pe.bump()] if last else (),
                       reads={"yb", "opw"}, writes={f"pb{bi}"})
            G[("opw", gp)] = s_pe.ev()
        ev_op = s_pe.ev()

        for q in range(2):
            for dmh in range(2):
                bi = 4 + q * 2 + dmh
                em.emit("vector", lambda e, q=q, dmh=dmh, bi=bi, par=par: e.tensor_add(
                    out=r1_rot(q)[:, dmh * 512:(dmh + 1) * 512],
                    in0=pb[bi].ap(),
                    in1=x_tok.ap()[:, (par * 2 + q) * 1024 + dmh * 512:
                                   (par * 2 + q) * 1024 + (dmh + 1) * 512]),
                    reads={f"pb{bi}", "x_tok"}, writes={"r1_rot"},
                    waits=[ev_op, gw(("r1", q))],
                    incs=[s_dve.bump()])
                G[f"pb{bi}"] = s_dve.ev()
            r1q = r1_rot(q)
            em.emit("vector", lambda e, r1q=r1q: e.reduce_sum(
                out=ln_s.ap()[:, 0:1], in_=r1q, axis=AX.X),
                reads={"r1_rot"}, writes={"ln_s"})
            em.emit("scalar", lambda e, r1q=r1q: e.activation(
                out=sq_rot.ap(), in_=r1q, func=AF.Square),
                reads={"r1_rot"}, writes={"sq_rot"},
                waits=[(s_dve, s_dve.count), gw("sq")],
                incs=[s_act.bump()])
            ev_sq = s_act.ev()
            em.emit("vector", lambda e: e.reduce_sum(
                out=ln_s.ap()[:, 1:2], in_=sq_rot.ap(), axis=AX.X),
                reads={"sq_rot"}, writes={"ln_s"}, waits=[ev_sq])
            em.emit("vector", lambda e: e.tensor_scalar(
                out=ln_s.ap()[:, 2:3], in0=ln_s.ap()[:, 0:1], scalar1=1.0 / D,
                scalar2=None, op0=ALU.mult),
                reads={"ln_s"}, writes={"ln_s"})
            em.emit("vector", lambda e: e.tensor_mul(
                out=ln_s.ap()[:, 3:4], in0=ln_s.ap()[:, 2:3], in1=ln_s.ap()[:, 2:3]),
                reads={"ln_s"}, writes={"ln_s"})
            em.emit("vector", lambda e: e.tensor_scalar(
                out=ln_s.ap()[:, 4:5], in0=ln_s.ap()[:, 1:2], scalar1=1.0 / D,
                scalar2=None, op0=ALU.mult),
                reads={"ln_s"}, writes={"ln_s"})
            em.emit("vector", lambda e: e.tensor_sub(
                out=ln_s.ap()[:, 5:6], in0=ln_s.ap()[:, 4:5], in1=ln_s.ap()[:, 3:4]),
                reads={"ln_s"}, writes={"ln_s"},
                incs=[s_dve.bump()])
            ev_var = s_dve.ev()
            G["sq"] = ev_var
            em.emit("scalar", lambda e: e.activation(
                out=ln_s.ap()[:, 6:7], in_=ln_s.ap()[:, 5:6], func=AF.Ln,
                bias=cst.ap()[:, 0:1]),
                reads={"ln_s"}, writes={"ln_s"}, waits=[ev_var])
            em.emit("scalar", lambda e: e.activation(
                out=ln_s.ap()[:, 7:8], in_=ln_s.ap()[:, 6:7], func=AF.Exp,
                scale=cst.ap()[:, 1:2]),
                reads={"ln_s"}, writes={"ln_s"},
                incs=[s_act.bump()])
            ev_rstd = s_act.ev()
            em.emit("vector", lambda e, q=q, r1q=r1q: e.tensor_scalar(
                out=r1n_rot(q), in0=r1q, scalar1=ln_s.ap()[:, 2:3],
                scalar2=ln_s.ap()[:, 7:8], op0=ALU.subtract, op1=ALU.mult),
                reads={"r1_rot", "ln_s"}, writes={"r1n_rot"},
                waits=[ev_rstd, gw(("r1n", q)), gw(("r1ntok", q))],
                incs=[s_dve.bump()])
            ev_r1n = s_dve.ev()
            G[("r1", q)] = ev_r1n
            t0 = c * TC + q * 128
            ev_wtok = dma(d_rtw[q], s_r1n[t0:t0 + 128, :], r1n_rot(q),
                          waits=[ev_r1n])
            for half in range(2):
                bsel = 2 + half
                for kd in range(4):
                    kdg = half * 4 + kd
                    em.emit("tensor", lambda e, q=q, kdg=kdg, kd=kd, bsel=bsel: e.transpose(
                        out=pb[bsel].ap()[:, kd * 128:(kd + 1) * 128],
                        in_=r1n_rot(q)[:, kdg * 128:(kdg + 1) * 128],
                        identity=ident_f.ap()),
                        reads={"r1n_rot", "ident_f"}, writes={f"pb{bsel}"},
                        waits=[ev_r1n, gw(f"pb{bsel}")] if kd == 0 else (),
                        incs=[s_pe.bump()] if kd == 3 else ())
                ev_tr2 = s_pe.ev()
                em.emit("scalar", lambda e, q=q, half=half, bsel=bsel: e.copy(
                    out=r1nT_rot(q)[:, half * 512:(half + 1) * 512],
                    in_=pb[bsel].ap()),
                    reads={f"pb{bsel}"}, writes={"r1nT_rot"},
                    waits=[ev_tr2, gw(("r1nT", q)) if half == 0 else None])
                em.emit("scalar", lambda e, q=q, half=half, bsel=bsel: e.copy(
                    out=gate_rot(q)[:, half * 512:(half + 1) * 512],
                    in_=pb[bsel].ap()),
                    reads={f"pb{bsel}"}, writes={"gate_rot"},
                    waits=[gw(("gate", q)) if half == 0 else None],
                    incs=[s_act.bump()])
                G[f"pb{bsel}"] = s_act.ev()
            ev_gtr = s_act.ev()
            G[("r1n", q)] = ev_gtr
            G[("r1ntok", q)] = ev_wtok
            for kd in range(KD):
                dma(d_rTw[q], s_r1nT[kd][:, t0:t0 + 128],
                    r1nT_rot(q)[:, kd * 128:(kd + 1) * 128],
                    waits=[ev_gtr] if kd == 0 else ())
            G[("r1nT", q)] = d_rTw[q].ev()
            for kd in range(KD):
                mm(pb[0].ap()[:, 0:8],
                   gate_rot(q)[:, kd * 128:(kd + 1) * 128],
                   wgr.ap()[:, kd * 8:(kd + 1) * 8],
                   start=(kd == 0), stop=(kd == KD - 1),
                   waits=[ev_gtr, gw("pb0")] if kd == 0 else (),
                   incs=[s_pe.bump()] if kd == KD - 1 else (),
                   reads={"gate_rot", "wgr"}, writes={"pb0"})
            ev_gmm = s_pe.ev()
            G[("gate", q)] = ev_gmm
            em.emit("scalar", lambda e: e.copy(
                out=lgs.ap()[:, 0:8], in_=pb[0].ap()[:, 0:8]),
                reads={"pb0"}, writes={"lgs"},
                waits=[ev_gmm, gw("lgs")],
                incs=[s_act.bump()])
            ev_lg = s_act.ev()
            G["pb0"] = ev_lg
            L = lgs.ap()
            em.emit("vector", lambda e, L=L: e.reduce_max(
                out=L[:, 8:9], in_=L[:, 0:8], axis=AX.X),
                reads={"lgs"}, writes={"lgs"}, waits=[ev_lg])
            em.emit("vector", lambda e, L=L: e.tensor_scalar(
                out=L[:, 16:24], in0=L[:, 0:8], scalar1=L[:, 8:9], scalar2=None,
                op0=ALU.is_equal),
                reads={"lgs"}, writes={"lgs"})
            em.emit("vector", lambda e, L=L: e.scalar_tensor_tensor(
                out=L[:, 24:32], in0=L[:, 16:24], scalar=-1e5, in1=L[:, 0:8],
                op0=ALU.mult, op1=ALU.add),
                reads={"lgs"}, writes={"lgs"})
            em.emit("vector", lambda e, L=L: e.reduce_max(
                out=L[:, 9:10], in_=L[:, 24:32], axis=AX.X),
                reads={"lgs"}, writes={"lgs"})
            em.emit("vector", lambda e, L=L: e.tensor_sub(
                out=L[:, 10:11], in0=L[:, 9:10], in1=L[:, 8:9]),
                reads={"lgs"}, writes={"lgs"},
                incs=[s_dve.bump()])
            ev_d = s_dve.ev()
            em.emit("scalar", lambda e, L=L: e.activation(
                out=L[:, 11:12], in_=L[:, 10:11], func=AF.Exp),
                reads={"lgs"}, writes={"lgs"}, waits=[ev_d],
                incs=[s_act.bump()])
            ev_e = s_act.ev()
            em.emit("vector", lambda e, L=L: e.tensor_scalar(
                out=L[:, 12:13], in0=L[:, 11:12], scalar1=1.0, scalar2=None,
                op0=ALU.add),
                reads={"lgs"}, writes={"lgs"}, waits=[ev_e])
            em.emit("vector", lambda e, L=L: e.reciprocal(
                out=L[:, 13:14], in_=L[:, 12:13]),
                reads={"lgs"}, writes={"lgs"})
            em.emit("vector", lambda e, L=L: e.tensor_mul(
                out=L[:, 14:15], in0=L[:, 11:12], in1=L[:, 13:14]),
                reads={"lgs"}, writes={"lgs"})
            em.emit("vector", lambda e, L=L: e.tensor_scalar(
                out=L[:, 32:40], in0=L[:, 24:32], scalar1=L[:, 9:10], scalar2=None,
                op0=ALU.is_equal),
                reads={"lgs"}, writes={"lgs"})
            em.emit("vector", lambda e, L=L: e.tensor_scalar(
                out=L[:, 40:48], in0=L[:, 16:24], scalar1=L[:, 13:14], scalar2=None,
                op0=ALU.mult),
                reads={"lgs"}, writes={"lgs"})
            tt = c * 2 + q
            em.emit("vector", lambda e, L=L, tt=tt: e.scalar_tensor_tensor(
                out=combine.ap()[:, tt * 8:(tt + 1) * 8], in0=L[:, 32:40],
                scalar=L[:, 14:15], in1=L[:, 40:48], op0=ALU.mult, op1=ALU.add),
                reads={"lgs"}, writes={"combine"},
                incs=[s_dve.bump()])
            G["lgs"] = s_dve.ev()
        G["yb"] = ev_op
        G[("x_tok", par)] = s_dve.ev()
        G["xT"] = ev_gmm
        G["bbcb"] = ev_yb

    a_pe = s_pe.ev()
    a_act = s_act.ev()
    a_dve = s_dve.ev()
    a_rtw = [d_rtw[0].ev(), d_rtw[1].ev()]
    a_rTw = [d_rTw[0].ev(), d_rTw[1].ev()]

    for g_ in reversed(ga):
        g_.__exit__(None, None, None)
    pg_s.__exit__(None, None, None)
    pg_b.__exit__(None, None, None)
    pg_f.__exit__(None, None, None)

    r1nT_sl = nc.alloc_sbuf_tensor("r1nT_sl", [128, KD * SLAB], BF16)
    w1w_t = nc.alloc_sbuf_tensor("w1w", [128, 2 * 1024], BF16)
    w1w = lambda i: w1w_t.ap()[:, i * 1024:(i + 1) * 1024]
    w2w_t = nc.alloc_sbuf_tensor("w2w", [128, 2 * 1024], BF16)
    w2w = lambda i: w2w_t.ap()[:, i * 1024:(i + 1) * 1024]
    w3w_t = nc.alloc_sbuf_tensor("w3w", [128, 2 * 1024], BF16)
    w3w = lambda i: w3w_t.ap()[:, i * 1024:(i + 1) * 1024]
    pprime = nc.alloc_sbuf_tensor("pprime", [128, NHK * SLAB], BF16)
    bs0_t = nc.alloc_sbuf_tensor("bs0", [128, 2 * SLAB], F32)
    bs0 = lambda i: bs0_t.ap()[:, i * SLAB:(i + 1) * SLAB]
    bs1_t = nc.alloc_sbuf_tensor("bs1", [128, 2 * SLAB], F32)
    bs1 = lambda i: bs1_t.ap()[:, i * SLAB:(i + 1) * SLAB]
    bs2_t = nc.alloc_sbuf_tensor("bs2", [128, 2 * SLAB], F32)
    bs2 = lambda i: bs2_t.ap()[:, i * SLAB:(i + 1) * SLAB]
    sH_t = nc.alloc_sbuf_tensor("sHB", [128, 2 * SLAB], F32)
    sH = lambda i: sH_t.ap()[:, i * SLAB:(i + 1) * SLAB]
    accB = nc.alloc_sbuf_tensor("accB", [128, 4 * 1024], F32)
    r1tok_t = nc.alloc_sbuf_tensor("r1tok", [128, 2 * 1024], F32)
    r1tok = lambda i: r1tok_t.ap()[:, i * 1024:(i + 1) * 1024]
    outr_t = nc.alloc_sbuf_tensor("outr", [128, 2 * 1024], F32)
    outr = lambda i: outr_t.ap()[:, i * 1024:(i + 1) * 1024]
    sqB = nc.alloc_sbuf_tensor("sqB", [128, 1024], F32)
    lnB = nc.alloc_sbuf_tensor("lnB", [128, 16], F32)

    barrier = [a_pe, a_act, a_dve, a_rtw[0], a_rtw[1], a_rTw[0], a_rTw[1],
               d_ms[0].ev(), d_ms[1].ev(), d_pl.ev()]

    for sl in range(NSL):
        t0 = sl * SLAB
        sp = sl % 2
        ev_r1t = dma(d_rr[sp],
                     r1nT_sl.ap().rearrange("p (kd t) -> p kd t", kd=KD),
                     s_r1nT.rearrange("kd p t -> p kd t")[:, :, t0:t0 + SLAB],
                     waits=(barrier if sl == 0 else [gw("r1nT_sl")]))
        for e_i in range(NE):
            w12ev = {}

            def ld12(hg, e_i=e_i):
                wp = hg % 2
                dma(d_wa[wp], w1w(wp), s_w1[e_i, hg], waits=[gw(("w1w", wp))])
                ev1 = d_wa[wp].ev()
                dma(d_wo[wp], w2w(wp), s_w2[e_i, hg], waits=[gw(("w2w", wp))])
                w12ev[hg] = (ev1, d_wo[wp].ev())
            ld12(0)
            ld12(1)
            for hg in range(NHK):
                wp = hg % 2
                sp_ = hg % 2
                ev_w1, ev_w2 = w12ev[hg]
                bH = hg % 2
                bG = 2 + hg % 2
                for k in range(KD):
                    mm(pb[bH].ap(),
                       w1w(wp)[:, k * 128:(k + 1) * 128],
                       r1nT_sl.ap()[:, k * SLAB:(k + 1) * SLAB],
                       start=(k == 0), stop=(k == KD - 1),
                       waits=[ev_w1, ev_r1t, gw(f"pb{bH}")] if k == 0 else (),
                       incs=[s_pe.bump()] if k == KD - 1 else (),
                       reads={"w1w", "r1nT_sl"}, writes={f"pb{bH}"})
                ev_H = s_pe.ev()
                G[("w1w", wp)] = ev_H
                for k in range(KD):
                    mm(pb[bG].ap(),
                       w2w(wp)[:, k * 128:(k + 1) * 128],
                       r1nT_sl.ap()[:, k * SLAB:(k + 1) * SLAB],
                       start=(k == 0), stop=(k == KD - 1),
                       waits=[ev_w2, gw(f"pb{bG}")] if k == 0 else (),
                       incs=[s_pe.bump()] if k == KD - 1 else (),
                       reads={"w2w", "r1nT_sl"}, writes={f"pb{bG}"})
                ev_G = s_pe.ev()
                G[("w2w", wp)] = ev_G
                if hg + 2 < NHK:
                    ld12(hg + 2)
                em.emit("scalar", lambda e, bH=bH, sp_=sp_: e.activation(
                    out=bs0(sp_), in_=pb[bH].ap(), func=AF.Exp, scale=-1.0),
                    reads={f"pb{bH}"}, writes={f"bs0{sp_}"},
                    waits=[ev_H, gw(("bsg", sp_))])
                em.emit("scalar", lambda e, sp_=sp_: e.activation(
                    out=bs1(sp_), in_=bs0(sp_), func=AF.Ln, bias=1.0),
                    reads={f"bs0{sp_}"}, writes={f"bs1{sp_}"})
                em.emit("scalar", lambda e, sp_=sp_: e.activation(
                    out=bs2(sp_), in_=bs1(sp_), func=AF.Exp, scale=-1.0),
                    reads={f"bs1{sp_}"}, writes={f"bs2{sp_}"},
                    incs=[s_act.bump()])
                ev_sg = s_act.ev()
                em.emit("vector", lambda e, bH=bH, sp_=sp_: e.tensor_mul(
                    out=sH(sp_), in0=bs2(sp_), in1=pb[bH].ap()),
                    reads={f"bs2{sp_}", f"pb{bH}"}, writes={f"sHB{sp_}"},
                    waits=[ev_sg],
                    incs=[s_dve.bump()])
                G[f"pb{bH}"] = s_dve.ev()
                G[("bsg", sp_)] = s_dve.ev()
                em.emit("vector", lambda e, hg=hg, bG=bG, sp_=sp_: e.tensor_mul(
                    out=pprime.ap()[:, hg * SLAB:(hg + 1) * SLAB],
                    in0=sH(sp_), in1=pb[bG].ap()),
                    reads={f"sHB{sp_}", f"pb{bG}"}, writes={"pprime"},
                    waits=[ev_G, gw("pprime") if hg == 0 else None],
                    incs=[s_dve.bump()])
                G[f"pb{bG}"] = s_dve.ev()
            ev_pp = s_dve.ev()
            for dmh in range(2):
                w3ev = {}

                def ld3(hk, e_i=e_i, w3ev=w3ev):
                    wp3 = hk % 2
                    dma(d_wb[wp3], w3w(wp3), s_w3[e_i, hk],
                        waits=[gw(("w3w", wp3))])
                    w3ev[hk] = d_wb[wp3].ev()
                ld3(0)
                ld3(1)
                for hk in range(NHK):
                    wp3 = hk % 2
                    for tt in range(4):
                        bsel = tt
                        mm(pb[bsel].ap(),
                           pprime.ap()[:, hk * SLAB + tt * 128:hk * SLAB + tt * 128 + 128],
                           w3w(wp3)[:, dmh * 512:(dmh + 1) * 512],
                           start=(hk == 0), stop=(hk == NHK - 1),
                           waits=([w3ev[hk], ev_pp, gw(f"pb{bsel}")]
                                  if hk == 0 else
                                  ([w3ev[hk]] if tt == 0 else ())),
                           incs=[s_pe.bump()] if tt == 3 else (),
                           reads={"pprime", "w3w"}, writes={f"pb{bsel}"})
                    G[("w3w", wp3)] = s_pe.ev()
                    if hk + 2 < NHK:
                        ld3(hk + 2)
                for tt in range(4):
                    ev_Y = s_pe.ev()
                    tt_g = sl * 4 + tt
                    ce = combine.ap()[:, tt_g * NE + e_i:tt_g * NE + e_i + 1]
                    acc_sl = accB.ap()[:, tt * 1024 + dmh * 512:tt * 1024 + (dmh + 1) * 512]
                    if e_i == 0:
                        em.emit("vector", lambda e, acc_sl=acc_sl, ce=ce, tt=tt: e.tensor_scalar(
                            out=acc_sl, in0=pb[tt].ap(), scalar1=ce, scalar2=None,
                            op0=ALU.mult),
                            reads={f"pb{tt}", "combine"}, writes={"accB"},
                            waits=[ev_Y, gw("accB") if (tt == 0 and dmh == 0) else None],
                            incs=[s_dve.bump()])
                    else:
                        em.emit("vector", lambda e, acc_sl=acc_sl, ce=ce, tt=tt: e.scalar_tensor_tensor(
                            out=acc_sl, in0=pb[tt].ap(), scalar=ce, in1=acc_sl,
                            op0=ALU.mult, op1=ALU.add),
                            reads={f"pb{tt}", "combine", "accB"}, writes={"accB"},
                            waits=[ev_Y],
                            incs=[s_dve.bump()])
                    G[f"pb{tt}"] = s_dve.ev()
            G["pprime"] = s_pe.ev()
        G["r1nT_sl"] = s_pe.ev()
        for tt in range(4):
            tp = tt % 2
            tg0 = t0 + tt * 128
            ev_r1 = dma(d_rt[tp], r1tok(tp), s_r1n[tg0:tg0 + 128, :],
                        waits=[gw(("r1tok", tp))])
            em.emit("vector", lambda e, tp=tp, tt=tt: e.tensor_add(
                out=outr(tp), in0=accB.ap()[:, tt * 1024:(tt + 1) * 1024],
                in1=r1tok(tp)),
                reads={"accB", "r1tok"}, writes={"outr"},
                waits=[ev_r1, gw(("outr", tp))],
                incs=[s_dve.bump()])
            G[("r1tok", tp)] = s_dve.ev()
            r2q = outr(tp)
            em.emit("vector", lambda e, r2q=r2q: e.reduce_sum(
                out=lnB.ap()[:, 0:1], in_=r2q, axis=AX.X),
                reads={"outr"}, writes={"lnB"})
            em.emit("scalar", lambda e, r2q=r2q: e.activation(
                out=sqB.ap(), in_=r2q, func=AF.Square),
                reads={"outr"}, writes={"sqB"},
                waits=[(s_dve, s_dve.count), gw("sqB")],
                incs=[s_act.bump()])
            ev_sqB = s_act.ev()
            em.emit("vector", lambda e: e.reduce_sum(
                out=lnB.ap()[:, 1:2], in_=sqB.ap(), axis=AX.X),
                reads={"sqB"}, writes={"lnB"}, waits=[ev_sqB])
            em.emit("vector", lambda e: e.tensor_scalar(
                out=lnB.ap()[:, 2:3], in0=lnB.ap()[:, 0:1], scalar1=1.0 / D,
                scalar2=None, op0=ALU.mult),
                reads={"lnB"}, writes={"lnB"})
            em.emit("vector", lambda e: e.tensor_mul(
                out=lnB.ap()[:, 3:4], in0=lnB.ap()[:, 2:3], in1=lnB.ap()[:, 2:3]),
                reads={"lnB"}, writes={"lnB"})
            em.emit("vector", lambda e: e.tensor_scalar(
                out=lnB.ap()[:, 4:5], in0=lnB.ap()[:, 1:2], scalar1=1.0 / D,
                scalar2=None, op0=ALU.mult),
                reads={"lnB"}, writes={"lnB"})
            em.emit("vector", lambda e: e.tensor_sub(
                out=lnB.ap()[:, 5:6], in0=lnB.ap()[:, 4:5], in1=lnB.ap()[:, 3:4]),
                reads={"lnB"}, writes={"lnB"},
                incs=[s_dve.bump()])
            ev_v = s_dve.ev()
            G["sqB"] = ev_v
            em.emit("scalar", lambda e: e.activation(
                out=lnB.ap()[:, 6:7], in_=lnB.ap()[:, 5:6], func=AF.Ln,
                bias=cst.ap()[:, 0:1]),
                reads={"lnB"}, writes={"lnB"}, waits=[ev_v])
            em.emit("scalar", lambda e: e.activation(
                out=lnB.ap()[:, 7:8], in_=lnB.ap()[:, 6:7], func=AF.Exp,
                scale=cst.ap()[:, 1:2]),
                reads={"lnB"}, writes={"lnB"},
                incs=[s_act.bump()])
            ev_rs = s_act.ev()
            em.emit("vector", lambda e, tp=tp, r2q=r2q: e.tensor_scalar(
                out=outr(tp), in0=r2q, scalar1=lnB.ap()[:, 2:3],
                scalar2=lnB.ap()[:, 7:8], op0=ALU.subtract, op1=ALU.mult),
                reads={"outr", "lnB"}, writes={"outr"}, waits=[ev_rs],
                incs=[s_dve.bump()])
            ev_o = s_dve.ev()
            G[("outr", tp)] = dma(d_out[tp], out_dr[tg0:tg0 + 128, :], outr(tp),
                                  waits=[ev_o])
        G["accB"] = s_dve.ev()

    em.emit("sync", lambda e: e.nop(), waits=[d_out[0].ev(), d_out[1].ev()])

    with nc.Block() as block:
        em.replay(block)
    return nc


def kernel(**inputs):
    x = np.ascontiguousarray(np.asarray(inputs["L0_attn_x"], dtype=np.float32))
    conv_w = np.ascontiguousarray(
        np.asarray(inputs["L0_attn_conv_w"], dtype=np.float32).reshape(DI, 4))
    common = {
        "x_proj_w": np.ascontiguousarray(np.asarray(inputs["L0_attn_x_proj_w"], np.float32)),
        "dt_proj_w": np.ascontiguousarray(np.asarray(inputs["L0_attn_dt_proj_w"], np.float32)),
        "dt_proj_b": np.ascontiguousarray(np.asarray(inputs["L0_attn_dt_proj_b"], np.float32)),
        "A_log": np.ascontiguousarray(np.asarray(inputs["L0_attn_A_log"], np.float32)),
        "Dvec": np.ascontiguousarray(np.asarray(inputs["L0_attn_D"], np.float32)),
        "B_proj_w": np.ascontiguousarray(np.asarray(inputs["L0_attn_B_proj_w"], np.float32)),
        "C_proj_w": np.ascontiguousarray(np.asarray(inputs["L0_attn_C_proj_w"], np.float32)),
        "conv_w": conv_w,
        "conv_b": np.ascontiguousarray(np.asarray(inputs["L0_attn_conv_b"], np.float32)),
        "out_proj_w": np.ascontiguousarray(np.asarray(inputs["L0_attn_out_proj_w"], np.float32)),
        "Wg": np.ascontiguousarray(np.asarray(inputs["L0_ffn_Wg"], np.float32)),
        "W1": np.ascontiguousarray(np.asarray(inputs["L0_ffn_W1"], np.float32)),
        "W2": np.ascontiguousarray(np.asarray(inputs["L0_ffn_W2"], np.float32)),
        "W3": np.ascontiguousarray(np.asarray(inputs["L0_ffn_W3"], np.float32)),
    }
    in_maps = []
    for i in range(8):
        m = {"x": np.ascontiguousarray(x[i])}
        m.update(common)
        in_maps.append(m)
    nc = build_program()
    res = run_bass_kernel_spmd(nc, in_maps, core_ids=list(range(8)))
    return np.stack([np.asarray(r["out"]) for r in res.results], axis=0)


if __name__ == "__main__":
    nc = build_program()
    print("built:", nc.n_instructions())


# version 10 (final)
# speedup vs baseline: 1.2067x; 1.2067x over previous
import numpy as np
import concourse.bass as bass
import concourse.mybir as mybir
from concourse.bass_utils import run_bass_kernel_spmd

F32 = mybir.dt.float32
BF16 = mybir.dt.bfloat16
AF = mybir.ActivationFunctionType
ALU = mybir.AluOpType
AX = mybir.AxisListType

import os
GPQ = os.environ.get("KGPQ", "sync")
KSKEW = int(os.environ.get("KSKEW", "1"))
KINPSCAN = int(os.environ.get("KINPSCAN", "1"))
KINPACT = int(os.environ.get("KINPACT", "1"))
KNCH = int(os.environ.get("KNCH", "16"))
KNSL = int(os.environ.get("KNSL", "8"))
KPACE = int(os.environ.get("KPACE", "1"))
KPREP = int(os.environ.get("KPREP", "3"))
KDBG = int(os.environ.get("KDBG", "0"))
T = 4096
D = 1024
DI = 2048
DS = 16
NG = 16
KD = 8
TC = 256
NCH = T // TC
SLAB = 512
NSL = T // SLAB
NE = 8
HID = 2048
NHK = 16
EPS = 1e-5


class Sem:
    def __init__(self, nc, name):
        self.name = name
        self._cm = nc.semaphore(name)
        self.h = self._cm.__enter__()
        self.count = 0

    def inc(self, n=1):
        self.count += n
        return self.count

    def bump(self):
        self.count += 1
        return (self, 1)

    def ev(self):
        return (self, self.count)


class Emitter:
    ENGS = ("sync", "tensor", "scalar", "vector", "gpsimd")

    def __init__(self):
        self.items = {e: [] for e in self.ENGS}
        self.hsem = {}

    def emit(self, eng, fn, reads=(), writes=(), waits=(), incs=()):
        ws = [w for w in waits if w is not None]
        for (s, v) in ws:
            assert v <= s.count, f"wait on future event {s.name}:{v}>{s.count}"
        evs_after = tuple((s, s.count) for (s, _a) in incs)
        self.items[eng].append((tuple(ws), fn, tuple(incs), evs_after,
                                frozenset(reads), frozenset(writes)))

    def replay(self, block):
        def mk(name):
            def body(eng):
                last_w = frozenset()
                last_r = frozenset()
                last_inst = [None]
                last_evs = [()]
                waited = {}
                hs = self.hsem.get(name)
                chk = name in ("scalar", "vector")
                for (waits, fn, incs, evs_after, reads, writes) in self.items[name]:
                    hazard = chk and (((reads | writes) & last_w)
                                      or (writes & last_r))
                    ws = list(waits)
                    if hazard and last_inst[0] is not None:
                        if last_evs[0]:
                            ws.append(last_evs[0][0])
                        else:
                            hs.count += 1
                            last_inst[0].then_inc(hs.h, 1)
                            ws.append((hs, hs.count))
                    for (s, v) in ws:
                        if waited.get(s.name, -1) >= v:
                            continue
                        eng.wait_ge(s.h, v)
                        waited[s.name] = v
                    inst = fn(eng)
                    for (s, v) in incs:
                        inst.then_inc(s.h, v)
                    last_inst[0] = inst
                    last_evs[0] = evs_after
                    last_w = writes
                    last_r = reads
            return body
        for n in self.ENGS:
            getattr(block, n)(mk(n))


def build_program():
    nc = bass.Bass("TRN2", target_bir_lowering=False, debug=False,
                   dynamic_dma_scratch_size=16384)

    x_dr = nc.dram_tensor("x", [T, D], F32, kind="ExternalInput").ap()
    xw_dr = nc.dram_tensor("x_proj_w", [DI, D], F32, kind="ExternalInput").ap()
    dtw_dr = nc.dram_tensor("dt_proj_w", [DI, D], F32, kind="ExternalInput").ap()
    dtb_dr = nc.dram_tensor("dt_proj_b", [DI], F32, kind="ExternalInput").ap()
    alog_dr = nc.dram_tensor("A_log", [DI, DS], F32, kind="ExternalInput").ap()
    dvec_dr = nc.dram_tensor("Dvec", [DI], F32, kind="ExternalInput").ap()
    bw_dr = nc.dram_tensor("B_proj_w", [DS, D], F32, kind="ExternalInput").ap()
    cw_dr = nc.dram_tensor("C_proj_w", [DS, D], F32, kind="ExternalInput").ap()
    convw_dr = nc.dram_tensor("conv_w", [DI, 4], F32, kind="ExternalInput").ap()
    convb_dr = nc.dram_tensor("conv_b", [DI], F32, kind="ExternalInput").ap()
    ow_dr = nc.dram_tensor("out_proj_w", [D, DI], F32, kind="ExternalInput").ap()
    wg_dr = nc.dram_tensor("Wg", [D, NE], F32, kind="ExternalInput").ap()
    w1_dr = nc.dram_tensor("W1", [NE, D, HID], F32, kind="ExternalInput").ap()
    w2_dr = nc.dram_tensor("W2", [NE, D, HID], F32, kind="ExternalInput").ap()
    w3_dr = nc.dram_tensor("W3", [NE, HID, D], F32, kind="ExternalInput").ap()
    out_dr = nc.dram_tensor("out", [T, D], F32, kind="ExternalOutput").ap()

    s_xpT = nc.dram_tensor("s_xpT", [NG, 128, KD * 128], BF16).ap()
    s_dtT = nc.dram_tensor("s_dtT", [NG, 128, KD * 128], BF16).ap()
    s_opT = nc.dram_tensor("s_opT", [NG, 128, D], BF16).ap()
    s_w1 = nc.dram_tensor("s_w1", [NE, NHK, 128, KD * 128], BF16,
                          kind="ExternalOutput" if KDBG else "Internal").ap()
    s_w2 = nc.dram_tensor("s_w2", [NE, NHK, 128, KD * 128], BF16).ap()
    s_w3 = nc.dram_tensor("s_w3", [NE, NHK, 128, D], BF16).ap()
    _k = "ExternalOutput" if KDBG else "Internal"
    s_r1n = nc.dram_tensor("s_r1n", [T, D], F32, kind=_k).ap()
    s_r1nT = nc.dram_tensor("s_r1nT", [KD, 128, T], BF16, kind=_k).ap()

    ident_np = np.eye(128, dtype=np.float32)
    basis_np = np.zeros((48, 32 * 128), np.float32)
    for s in range(DS):
        basis_np[s, s * 128:(s + 1) * 128] = 1.0
        basis_np[32 + s, (16 + s) * 128:(17 + s) * 128] = 1.0
    ident_dr = nc.inline_tensor(ident_np, "ident_c").ap()
    basis_dr = nc.inline_tensor(basis_np, "basis_c").ap()

    em = Emitter()
    G = {}

    d_cst = Sem(nc, "d_cst")
    d_pl = Sem(nc, "d_pl")
    d_ps = Sem(nc, "d_ps")
    d_ms = Sem(nc, "d_ms")
    d_x = [Sem(nc, "d_x0"), Sem(nc, "d_x1")]
    d_wa = [Sem(nc, "d_wa0"), Sem(nc, "d_wa1")]
    d_wo = [Sem(nc, "d_wo0"), Sem(nc, "d_wo1")]
    d_wb = [Sem(nc, "d_wb0"), Sem(nc, "d_wb1")]
    d_rtw = [Sem(nc, "d_rtw0"), Sem(nc, "d_rtw1")]
    d_rTw = [Sem(nc, "d_rTw0"), Sem(nc, "d_rTw1")]
    d_rr = [Sem(nc, "d_rr0"), Sem(nc, "d_rr1")]
    d_rt = [Sem(nc, "d_rt0"), Sem(nc, "d_rt1")]
    d_out = [Sem(nc, "d_out0"), Sem(nc, "d_out1")]
    s_pe = Sem(nc, "s_pe")
    s_act = Sem(nc, "s_act")
    s_dve = Sem(nc, "s_dve")
    em.hsem["scalar"] = Sem(nc, "h_act")
    em.hsem["vector"] = Sem(nc, "h_dve")

    def dma(sem, out, in_, waits=(), slow=False, eng="sync"):
        sem.inc(16)
        em.emit(eng,
                lambda e, o=out, i=in_, s=slow: e.dma_start(
                    out=o, in_=i, allow_slow_non_contiguous=s),
                waits=waits, incs=[(sem, 16)], writes={"dma"})
        return sem.ev()

    def gw(key):
        return G.get(key)

    def mm(out_ap, lhsT, rhs, start, stop, waits=(), incs=(), reads=(), writes=()):
        em.emit("tensor", lambda e: e.matmul(out=out_ap, lhsT=lhsT, rhs=rhs,
                                             start=start, stop=stop),
                reads=set(reads), writes=set(writes), waits=waits, incs=incs)

    ident_f = nc.alloc_sbuf_tensor("ident_f", [128, 128], F32)
    ident_b = nc.alloc_sbuf_tensor("ident_b", [128, 128], BF16)
    basis_b = nc.alloc_sbuf_tensor("basis_b", [48, 32 * 128], BF16)
    acst = nc.alloc_sbuf_tensor("acst", [128, NG * DS], F32)
    dhalf = nc.alloc_sbuf_tensor("dhalf", [128, NG], F32)
    convw_sb = nc.alloc_sbuf_tensor("convw_sb", [128, NG * 4], F32)
    convb_sb = nc.alloc_sbuf_tensor("convb_sb", [128, NG], F32)
    dtb_sb = nc.alloc_sbuf_tensor("dtb_sb", [128, NG], F32)
    wgr = nc.alloc_sbuf_tensor("wgr", [128, KD * NE], F32)
    bcT = nc.alloc_sbuf_tensor("bcT", [128, KD * 48], BF16)
    combine = nc.alloc_sbuf_tensor("combine", [128, (T // 128) * NE], F32)
    carry = nc.alloc_sbuf_tensor("carry", [128, NG * DS], F32)
    tails = nc.alloc_sbuf_tensor("tails", [128, NG * 4], F32)
    cst = nc.alloc_sbuf_tensor("cst", [128, 2], F32)

    pb = [nc.alloc_psum_tensor(f"pb{i}", [128, 512], F32) for i in range(8)]

    pg_f = nc.sbuf_tensor("ptmp_f", [128, 2048], F32)
    ptmp_f = pg_f.__enter__()
    pg_b = nc.sbuf_tensor("ptmp_b", [128, 2048], BF16)
    ptmp_b = pg_b.__enter__()
    pg_s = nc.sbuf_tensor("pstage", [128, 2048], BF16)
    pstage = pg_s.__enter__()
    pg_bf = nc.sbuf_tensor("basis_f", [48, 32 * 128], F32)
    basis_f = pg_bf.__enter__()

    dma(d_cst, acst.ap().rearrange("p (g s) -> p g s", g=NG),
        alog_dr.rearrange("(g p) s -> p g s", p=128))
    dma(d_cst, dhalf.ap(), dvec_dr.rearrange("(g p) -> p g", p=128), slow=True)
    dma(d_cst, convw_sb.ap().rearrange("p (g k) -> p g k", g=NG),
        convw_dr.rearrange("(g p) k -> p g k", p=128))
    dma(d_cst, convb_sb.ap(), convb_dr.rearrange("(g p) -> p g", p=128), slow=True)
    dma(d_cst, dtb_sb.ap(), dtb_dr.rearrange("(g p) -> p g", p=128), slow=True)
    dma(d_cst, wgr.ap().rearrange("p (k e) -> p k e", k=KD),
        wg_dr.rearrange("(k p) e -> p k e", p=128))
    dma(d_cst, ident_f.ap(), ident_dr)
    dma(d_cst, basis_f.ap(), basis_dr)
    dma(d_cst, ptmp_f.ap()[0:16, 0:1024], bw_dr)
    dma(d_cst, ptmp_f.ap()[16:32, 0:1024], cw_dr)
    ev_cstall = d_cst.ev()

    em.emit("scalar", lambda e: e.activation(out=acst.ap(), in_=acst.ap(), func=AF.Exp),
            reads={"acst"}, writes={"acst"}, waits=[ev_cstall],
            incs=[s_act.bump()])
    em.emit("vector", lambda e: e.tensor_scalar(
        out=acst.ap(), in0=acst.ap(), scalar1=-1.0, scalar2=None, op0=ALU.mult),
        reads={"acst"}, writes={"acst"}, waits=[s_act.ev()])
    em.emit("vector", lambda e: e.tensor_scalar(
        out=dhalf.ap(), in0=dhalf.ap(), scalar1=0.5, scalar2=None, op0=ALU.mult),
        reads={"dhalf"}, writes={"dhalf"}, waits=[ev_cstall])
    em.emit("scalar", lambda e: e.copy(out=ident_b.ap(), in_=ident_f.ap()),
            reads={"ident_f"}, writes={"ident_b"}, waits=[ev_cstall],
            incs=[s_act.bump()])
    ev_identb = s_act.ev()
    em.emit("scalar", lambda e: e.copy(out=basis_b.ap(), in_=basis_f.ap()),
            reads={"basis_f"}, writes={"basis_b"}, waits=[ev_cstall])
    em.emit("vector", lambda e: e.memset(cst.ap()[:, 0:1], EPS), writes={"cst"})
    em.emit("vector", lambda e: e.memset(cst.ap()[:, 1:2], -0.5), writes={"cst"})
    em.emit("vector", lambda e: e.memset(carry.ap(), 0.0), writes={"carry"})
    em.emit("vector", lambda e: e.memset(tails.ap(), 0.0), writes={"tails"})
    em.emit("vector", lambda e: e.memset(bcT.ap(), 0.0), writes={"bcT"},
            incs=[s_dve.bump()])
    ev_bcz = s_dve.ev()

    em.emit("scalar", lambda e: e.copy(out=ptmp_b.ap()[0:32, 0:1024],
                                       in_=ptmp_f.ap()[0:32, 0:1024]),
            reads={"ptmp_f"}, writes={"ptmp_b"}, waits=[ev_cstall],
            incs=[s_act.bump()])
    ev_cast = s_act.ev()
    G["ptmp_f"] = ev_cast
    for k in range(KD):
        em.emit("tensor", lambda e, k=k: e.transpose(
            out=pb[7].ap().bitcast(BF16)[:, 0:32],
            in_=ptmp_b.ap()[0:32, k * 128:(k + 1) * 128],
            identity=ident_b.ap()[0:32, 0:32]),
            reads={"ptmp_b", "ident_b"}, writes={"pb7"},
            waits=[ev_cast, ev_identb, gw("pb7")],
            incs=[s_pe.bump()])
        ev_t = s_pe.ev()
        em.emit("scalar", lambda e, k=k: e.copy(
            out=bcT.ap()[:, k * 48:k * 48 + 16],
            in_=pb[7].ap().bitcast(BF16)[:, 0:16]),
            reads={"pb7"}, writes={"bcT"}, waits=[ev_t, ev_bcz])
        em.emit("scalar", lambda e, k=k: e.copy(
            out=bcT.ap()[:, k * 48 + 32:k * 48 + 48],
            in_=pb[7].ap().bitcast(BF16)[:, 16:32]),
            reads={"pb7"}, writes={"bcT"},
            incs=[s_act.bump()])
        G["pb7"] = s_act.ev()

    def prep_projT(src_dr, dst_dr):
        for g in range(NG):
            ev_l = dma(d_pl, ptmp_f.ap()[:, 0:1024],
                       src_dr[g * 128:(g + 1) * 128, :], waits=[gw("ptmp_f")])
            em.emit("scalar", lambda e: e.copy(out=ptmp_b.ap()[:, 0:1024],
                                               in_=ptmp_f.ap()[:, 0:1024]),
                    reads={"ptmp_f"}, writes={"ptmp_b"},
                    waits=[ev_l, gw("ptmp_b")],
                    incs=[s_act.bump()])
            G["ptmp_f"] = s_act.ev()
            ev_c = s_act.ev()
            for k in range(KD):
                em.emit("tensor", lambda e, k=k: e.transpose(
                    out=pb[7].ap().bitcast(BF16)[:, k * 128:(k + 1) * 128],
                    in_=ptmp_b.ap()[:, k * 128:(k + 1) * 128],
                    identity=ident_b.ap()),
                    reads={"ptmp_b"}, writes={"pb7"},
                    waits=[ev_c, gw("pb7")] if k == 0 else (),
                    incs=[s_pe.bump()] if k == KD - 1 else ())
            ev_t = s_pe.ev()
            em.emit("scalar", lambda e: e.copy(
                out=pstage.ap()[:, 0:1024],
                in_=pb[7].ap().bitcast(BF16)[:, 0:1024]),
                reads={"pb7"}, writes={"pstage"},
                waits=[ev_t, gw("pstage")],
                incs=[s_act.bump()])
            G["pb7"] = s_act.ev()
            G["ptmp_b"] = s_act.ev()
            ev_s = s_act.ev()
            G["pstage"] = dma(d_ps, dst_dr[g], pstage.ap()[:, 0:1024],
                              waits=[ev_s])

    if KPREP >= 2:
        prep_projT(xw_dr, s_xpT)
        prep_projT(dtw_dr, s_dtT)

    for k in range(KD if KPREP >= 2 else 0):
        ev_l = dma(d_pl, ptmp_f.ap(), ow_dr[k * 128:(k + 1) * 128, :],
                   waits=[gw("ptmp_f")])
        em.emit("scalar", lambda e: e.copy(out=ptmp_b.ap(), in_=ptmp_f.ap()),
                reads={"ptmp_f"}, writes={"ptmp_b"},
                waits=[ev_l, gw("ptmp_b")],
                incs=[s_act.bump()])
        G["ptmp_f"] = s_act.ev()
        ev_c = s_act.ev()
        for half in range(2):
            for gg in range(8):
                g = half * 8 + gg
                em.emit("tensor", lambda e, g=g, gg=gg: e.transpose(
                    out=pb[7].ap().bitcast(BF16)[:, gg * 128:(gg + 1) * 128],
                    in_=ptmp_b.ap()[:, g * 128:(g + 1) * 128],
                    identity=ident_b.ap()),
                    reads={"ptmp_b"}, writes={"pb7"},
                    waits=[ev_c, gw("pb7")] if gg == 0 else (),
                    incs=[s_pe.bump()] if gg == 7 else ())
            ev_t = s_pe.ev()
            em.emit("scalar", lambda e, half=half: e.copy(
                out=pstage.ap()[:, half * 1024:(half + 1) * 1024],
                in_=pb[7].ap().bitcast(BF16)[:, 0:1024]),
                reads={"pb7"}, writes={"pstage"},
                waits=[ev_t, gw("pstage") if half == 0 else None],
                incs=[s_act.bump()])
            G["pb7"] = s_act.ev()
        G["ptmp_b"] = s_act.ev()
        ev_s = s_act.ev()
        evs = None
        for g in range(NG):
            evs = dma(d_ps, s_opT[g][:, k * 128:(k + 1) * 128],
                      pstage.ap()[:, g * 128:(g + 1) * 128],
                      waits=[ev_s] if g == 0 else ())
        G["pstage"] = evs

    prep_pe = s_pe.ev()
    prep_act = s_act.ev()
    prep_dma = d_ps.ev()
    prep_dma2 = d_pl.ev()

    d_msb = [Sem(nc, "d_msb0"), Sem(nc, "d_msb1")]
    _units = []
    for e_i in range(NE):
        for hg in range(NHK):
            src = w1_dr[e_i].rearrange("(k p) (hg c) -> hg p k c", p=128, c=128)[hg]
            _units.append((s_w1[e_i, hg].rearrange("p (k c) -> p k c", k=KD), src))
            src = w2_dr[e_i].rearrange("(k p) (hg c) -> hg p k c", p=128, c=128)[hg]
            _units.append((s_w2[e_i, hg].rearrange("p (k c) -> p k c", k=KD), src))
        for hk in range(NHK):
            _units.append((s_w3[e_i, hk], w3_dr[e_i, hk * 128:(hk + 1) * 128, :]))
    if KPREP >= 3:
        for bi in range(0, len(_units), 8):
            bsem = d_msb[(bi // 8) % 2]
            if bi == 0:
                w = [prep_dma, prep_dma2, ev_cstall]
            elif KPACE and bi >= 16:
                w = [bsem.ev()]
            else:
                w = ()
            for j, (o, i) in enumerate(_units[bi:bi + 8]):
                dma(bsem, o, i, eng="gpsimd", waits=w if j == 0 else ())
    moe_prep_done = [d_msb[0].ev(), d_msb[1].ev()]



    pg_bf.__exit__(None, None, None)
    pg_s.__exit__(None, None, None)
    pg_b.__exit__(None, None, None)
    pg_f.__exit__(None, None, None)

    ga = []

    def A(name, shape, dt):
        g_ = nc.sbuf_tensor(name, shape, dt)
        t = g_.__enter__()
        ga.append(g_)
        return t

    def slot2(name, w, dt):
        t = A(name, [128, 2 * w], dt)
        return (lambda i, t=t, w=w: t.ap()[:, i * w:(i + 1) * w])

    x_tok = slot2("x_tok", 2048, F32)
    xbf = A("xbf", [128, 2048], BF16)
    xT = slot2("xT", 2048, BF16)
    xpw = slot2("xpw", 1024, BF16)
    dtw_sb = slot2("dtw_sb", 1024, BF16)
    opw = slot2("opw", 1024, BF16)
    xi_rot = slot2("xi_rot", 264, F32)
    cacc = slot2("cacc", TC, F32)
    sgm = slot2("sgm", TC, F32)
    sp_rot = A("sp_rot", [128, TC], F32)
    sgt1 = None if KINPACT else slot2("sgt1", TC, F32)
    sgt2 = None if KINPACT else slot2("sgt2", TC, F32)
    dl = slot2("delta", NG * TC, BF16)
    xcs = slot2("xc", NG * TC, BF16)
    dxs = slot2("dtx", NG * TC, BF16)
    bc_f = A("bc_f", [48, TC], BF16)
    bb = A("bb", [128, DS * TC], BF16)
    cb = A("cb", [128, DS * TC], BF16)
    a_buf = slot2("a_buf", 8 * TC, F32)
    u_buf = A("u_buf", [128, 8 * TC], BF16)
    h_sep = None if KINPSCAN else A("h_sep", [128, 8 * TC], BF16)
    hc_st = A("hc_st", [128, DS * TC], BF16)
    treeF = A("treeF", [128, 2 * TC], F32)
    ytmp = A("ytmp", [128, TC], F32)
    xcd = A("xcd", [128, TC], F32)
    bndt = A("bndt", [128, 16], F32)
    yb = A("yb", [128, NG * TC], BF16)
    r1_rot = slot2("r1_rot", 1024, F32)
    sq_rot = A("sq_rot", [128, 1024], F32)
    r1nT_rot = slot2("r1nT_rot", 1024, BF16)
    gate_rot = slot2("gate_rot", 1024, F32)
    lgs = A("lgs", [128, 64], F32)
    ln_s = A("ln_s", [128, 16], F32)

    x3 = x_dr.rearrange("(c q p) d -> c q p d", q=2, p=128)

    def emit_xc_dtx(c, g, ev_sg):
        par = c % 2
        gp = g % 2
        em.emit("vector", lambda e, gp=gp, g=g, par=par: e.scalar_tensor_tensor(
            out=xcs(par)[:, g * TC:(g + 1) * TC], in0=sgm(gp), scalar=2.0,
            in1=cacc(gp), op0=ALU.mult, op1=ALU.mult),
            reads={f"sgm{gp}", "cacc"}, writes={"xc"}, waits=[ev_sg])
        em.emit("vector", lambda e, g=g, par=par: e.tensor_mul(
            out=dxs(par)[:, g * TC:(g + 1) * TC],
            in0=dl(par)[:, g * TC:(g + 1) * TC],
            in1=xcs(par)[:, g * TC:(g + 1) * TC]),
            reads={"delta", "xc"}, writes={"dtx"},
            incs=[s_dve.bump()])
        G[("cacc", gp)] = s_dve.ev()
        G[("sgm2", gp)] = s_dve.ev()

    def mm_phase(c):
        par = c % 2
        for q in range(2):
            dma(d_x[par], x_tok(par)[:, q * 1024:(q + 1) * 1024], x3[c, q],
                waits=[gw(("x_tok", par)),
                       prep_dma if (c == 0 and q == 0) else None,
                       prep_dma2 if (c == 0 and q == 0) else None])
        ev_x = d_x[par].ev()
        em.emit("scalar", lambda e, par=par: e.copy(
            out=xbf.ap(), in_=x_tok(par)),
            reads={"x_tok"}, writes={"xbf"},
            waits=[ev_x, prep_act if c == 0 else None, gw("xbf")],
            incs=[s_act.bump()])
        ev_xbf = s_act.ev()
        ev_tr = None
        for q in range(2):
            bk = pb[q].ap().bitcast(BF16)
            for kd in range(KD):
                em.emit("tensor", lambda e, q=q, kd=kd, bk=bk: e.transpose(
                    out=bk[:, kd * 128:(kd + 1) * 128],
                    in_=xbf.ap()[:, q * 1024 + kd * 128:q * 1024 + (kd + 1) * 128],
                    identity=ident_b.ap()),
                    reads={"xbf"}, writes={f"pb{q}"},
                    waits=[ev_xbf, prep_pe if c == 0 else None,
                           gw(f"pb{q}")] if kd == 0 else (),
                    incs=[s_pe.bump()] if kd == KD - 1 else ())
            ev_tr = s_pe.ev()
            dst = xT(par).rearrange("p (kd t) -> p kd t", kd=KD)[:, :, q * 128:(q + 1) * 128]
            em.emit("scalar", lambda e, bk=bk, dst=dst: e.copy(
                out=dst, in_=bk.rearrange("p (kd t) -> p kd t", kd=KD)),
                reads={f"pb{q}"}, writes={"xT"},
                waits=[ev_tr, gw(("xT", par)) if q == 0 else None],
                incs=[s_act.bump()])
            G[f"pb{q}"] = s_act.ev()
        ev_xT = s_act.ev()
        G["xbf"] = ev_tr

        ev_sg_prev = None
        for g in range(NG):
            gp = g % 2
            bxi = [0, 2][gp]
            bdt = [1, 3][gp]
            dma(d_wa[gp], xpw(gp), s_xpT[g], waits=[gw(("xpw", gp))])
            dma(d_wa[gp], dtw_sb(gp), s_dtT[g], waits=[gw(("dtw", gp))])
            ev_w = d_wa[gp].ev()
            for k in range(KD):
                mm(pb[bxi].ap()[:, 0:TC],
                   xpw(gp)[:, k * 128:(k + 1) * 128],
                   xT(par)[:, k * TC:(k + 1) * TC],
                   start=(k == 0), stop=(k == KD - 1),
                   waits=[ev_w, ev_xT, gw(f"pb{bxi}")] if k == 0 else (),
                   incs=[s_pe.bump()] if k == KD - 1 else (),
                   reads={"xpw", "xT"}, writes={f"pb{bxi}"})
            ev_xi = s_pe.ev()
            G[("xpw", gp)] = ev_xi
            for k in range(KD):
                mm(pb[bdt].ap()[:, 0:TC],
                   dtw_sb(gp)[:, k * 128:(k + 1) * 128],
                   xT(par)[:, k * TC:(k + 1) * TC],
                   start=(k == 0), stop=(k == KD - 1),
                   waits=[ev_w, gw(f"pb{bdt}")] if k == 0 else (),
                   incs=[s_pe.bump()] if k == KD - 1 else (),
                   reads={"dtw_sb", "xT"}, writes={f"pb{bdt}"})
            ev_dt = s_pe.ev()
            G[("dtw", gp)] = ev_dt
            em.emit("scalar", lambda e, gp=gp, bxi=bxi: e.copy(
                out=xi_rot(gp)[:, 3:3 + TC], in_=pb[bxi].ap()[:, 0:TC]),
                reads={f"pb{bxi}"}, writes={"xi_rot"},
                waits=[ev_xi, gw(("xi", gp))])
            em.emit("scalar", lambda e, gp=gp, g=g: e.copy(
                out=xi_rot(gp)[:, 0:3], in_=tails.ap()[:, g * 4:g * 4 + 3]),
                reads={"tails"}, writes={"xi_rot"})
            em.emit("scalar", lambda e, gp=gp, g=g, bxi=bxi: e.copy(
                out=tails.ap()[:, g * 4:g * 4 + 3], in_=pb[bxi].ap()[:, TC - 3:TC]),
                reads={f"pb{bxi}"}, writes={"tails"},
                incs=[s_act.bump()])
            ev_xiev = s_act.ev()
            G[f"pb{bxi}"] = ev_xiev
            em.emit("scalar", lambda e, g=g, bdt=bdt: e.activation(
                out=sp_rot.ap(), in_=pb[bdt].ap()[:, 0:TC], func=AF.Exp,
                bias=dtb_sb.ap()[:, g:g + 1]),
                reads={f"pb{bdt}"}, writes={"sp_rot"},
                waits=[ev_dt])
            em.emit("scalar", lambda e, g=g, par=par: e.activation(
                out=dl(par)[:, g * TC:(g + 1) * TC], in_=sp_rot.ap(),
                func=AF.Ln, bias=1.0),
                reads={"sp_rot"}, writes={"delta"},
                incs=[s_act.bump()])
            ev_delta = s_act.ev()
            G[f"pb{bdt}"] = ev_delta
            em.emit("vector", lambda e, gp=gp, g=g: e.tensor_scalar(
                out=cacc(gp), in0=xi_rot(gp)[:, 0:TC],
                scalar1=convw_sb.ap()[:, g * 4:g * 4 + 1],
                scalar2=convb_sb.ap()[:, g:g + 1], op0=ALU.mult, op1=ALU.add),
                reads={"xi_rot"}, writes={"cacc"},
                waits=[ev_xiev, gw(("cacc", gp))])
            for k in range(1, 4):
                em.emit("vector", lambda e, gp=gp, g=g, k=k: e.scalar_tensor_tensor(
                    out=cacc(gp), in0=xi_rot(gp)[:, k:k + TC],
                    scalar=convw_sb.ap()[:, g * 4 + k:g * 4 + k + 1],
                    in1=cacc(gp), op0=ALU.mult, op1=ALU.add),
                    reads={"xi_rot", "cacc"}, writes={"cacc"})
            em.emit("vector", lambda e, gp=gp: e.tensor_scalar(
                out=sgm(gp), in0=cacc(gp), scalar1=-43.0, scalar2=None,
                op0=ALU.max),
                reads={"cacc"}, writes={f"sgm{gp}"},
                waits=[gw(("sgm2", gp))],
                incs=[s_dve.bump()])
            ev_clamp = s_dve.ev()
            G[("xi", gp)] = ev_clamp
            if g > 0:
                emit_xc_dtx(c, g - 1, ev_sg_prev)
            if KINPACT:
                em.emit("scalar", lambda e, gp=gp: e.activation(
                    out=sgm(gp), in_=sgm(gp), func=AF.Exp, scale=-1.0),
                    reads={f"sgm{gp}"}, writes={f"sgm{gp}"},
                    waits=[ev_clamp])
                em.emit("scalar", lambda e, gp=gp: e.activation(
                    out=sgm(gp), in_=sgm(gp), func=AF.Ln, bias=1.0),
                    reads={f"sgm{gp}"}, writes={f"sgm{gp}"})
                em.emit("scalar", lambda e, gp=gp: e.activation(
                    out=sgm(gp), in_=sgm(gp), func=AF.Exp, scale=-1.0),
                    reads={f"sgm{gp}"}, writes={f"sgm{gp}"},
                    incs=[s_act.bump()])
            else:
                em.emit("scalar", lambda e, gp=gp: e.activation(
                    out=sgt1(gp), in_=sgm(gp), func=AF.Exp, scale=-1.0),
                    reads={f"sgm{gp}"}, writes={f"sgt1{gp}"},
                    waits=[ev_clamp])
                em.emit("scalar", lambda e, gp=gp: e.activation(
                    out=sgt2(gp), in_=sgt1(gp), func=AF.Ln, bias=1.0),
                    reads={f"sgt1{gp}"}, writes={f"sgt2{gp}"})
                em.emit("scalar", lambda e, gp=gp: e.activation(
                    out=sgm(gp), in_=sgt2(gp), func=AF.Exp, scale=-1.0),
                    reads={f"sgt2{gp}"}, writes={f"sgm{gp}"},
                    incs=[s_act.bump()])
            ev_sg_prev = s_act.ev()
        emit_xc_dtx(c, NG - 1, ev_sg_prev)

    def scan_phase(c):
        par = c % 2
        for k in range(KD):
            mm(pb[4].ap()[0:48, 0:TC],
               bcT.ap()[:, k * 48:(k + 1) * 48],
               xT(par)[:, k * TC:(k + 1) * TC],
               start=(k == 0), stop=(k == KD - 1),
               waits=[gw("pb4")] if k == 0 else (),
               incs=[s_pe.bump()] if k == KD - 1 else (),
               reads={"bcT", "xT"}, writes={"pb4"})
        ev_bc = s_pe.ev()
        G[("xT", par)] = ev_bc
        em.emit("scalar", lambda e: e.copy(out=bc_f.ap(), in_=pb[4].ap()[0:48, 0:TC]),
                reads={"pb4"}, writes={"bc_f"},
                waits=[ev_bc, gw("bc_f")],
                incs=[s_act.bump()])
        ev_bcf = s_act.ev()
        G["pb4"] = ev_bcf
        for j in range(32):
            bsel = 4 + (j % 2)
            mm(pb[bsel].ap()[:, 0:TC],
               basis_b.ap()[:, j * 128:(j + 1) * 128],
               bc_f.ap(),
               start=True, stop=True,
               waits=[ev_bcf, gw(f"pb{bsel}"),
                      gw("bbcb") if j < 2 else None],
               incs=[s_pe.bump()],
               reads={"basis_b", "bc_f"}, writes={f"pb{bsel}"})
            ev_bmm = s_pe.ev()
            if j < 16:
                em.emit("scalar", lambda e, j=j, bsel=bsel: e.activation(
                    out=bb.ap()[:, j * TC:(j + 1) * TC],
                    in_=pb[bsel].ap()[:, 0:TC], func=AF.Copy, scale=0.5),
                    reads={f"pb{bsel}"}, writes={"bb"}, waits=[ev_bmm],
                    incs=[s_act.bump()])
            else:
                em.emit("scalar", lambda e, j=j, bsel=bsel: e.copy(
                    out=cb.ap()[:, (j - 16) * TC:(j - 15) * TC],
                    in_=pb[bsel].ap()[:, 0:TC]),
                    reads={f"pb{bsel}"}, writes={"cb"}, waits=[ev_bmm],
                    incs=[s_act.bump()])
            G[f"pb{bsel}"] = s_act.ev()
        ev_bbcb = s_act.ev()
        G["bc_f"] = ev_bbcb

        for g in range(NG):
            for sh in range(2):
                ap_ = (2 * g + sh) % 2
                for si in range(8):
                    sgl = sh * 8 + si
                    em.emit("scalar", lambda e, g=g, si=si, sgl=sgl, ap_=ap_, par=par: e.activation(
                        out=a_buf(ap_)[:, si * TC:(si + 1) * TC],
                        in_=dl(par)[:, g * TC:(g + 1) * TC], func=AF.Exp,
                        scale=acst.ap()[:, g * DS + sgl:g * DS + sgl + 1]),
                        reads={"delta"}, writes={f"a{ap_}"},
                        waits=[gw(("a", ap_))] if si == 0 else (),
                        incs=[s_act.bump()] if si == 7 else ())
                ev_a = s_act.ev()
                dtx_b3 = dxs(par)[:, g * TC:(g + 1) * TC].unsqueeze(1).broadcast_to([128, 8, TC])
                bb_3 = bb.ap().rearrange("p (s t) -> p s t", s=DS)[:, sh * 8:(sh + 1) * 8, :]
                u_3 = u_buf.ap().rearrange("p (s t) -> p s t", s=8)
                em.emit("vector", lambda e, dtx_b3=dtx_b3, bb_3=bb_3, u_3=u_3: e.tensor_tensor(
                    out=u_3, in0=dtx_b3, in1=bb_3, op=ALU.mult),
                    reads={"dtx", "bb"}, writes={"u_buf"},
                    waits=[ev_bbcb, ev_a])
                a_bnd = a_buf(ap_).rearrange("p (s t) -> p s t", s=8)[:, :, 0]
                u_bnd = u_buf.ap().rearrange("p (s t) -> p s t", s=8)[:, :, 0]
                cr = carry.ap()[:, g * DS + sh * 8:g * DS + sh * 8 + 8]
                em.emit("vector", lambda e, a_bnd=a_bnd, cr=cr: e.tensor_mul(
                    out=bndt.ap()[:, 0:8], in0=a_bnd, in1=cr),
                    reads={f"a{ap_}", "carry"}, writes={"bndt"})
                em.emit("vector", lambda e, u_bnd=u_bnd: e.tensor_add(
                    out=u_bnd, in0=u_bnd, in1=bndt.ap()[:, 0:8]),
                    reads={"u_buf", "bndt"}, writes={"u_buf"})
                em.emit("vector", lambda e, a_bnd=a_bnd: e.memset(a_bnd, 0.0),
                        writes={f"a{ap_}"})
                h_out = u_buf if KINPSCAN else h_sep
                em.emit("vector", lambda e, ap_=ap_, h_out=h_out: e.tensor_tensor_scan(
                    out=h_out.ap(), data0=a_buf(ap_), data1=u_buf.ap(),
                    initial=0.0, op0=ALU.mult, op1=ALU.add),
                    reads={f"a{ap_}", "u_buf"}, writes={"u_buf", "h_sep"})
                h_last = h_out.ap().rearrange("p (s t) -> p s t", s=8)[:, :, TC - 1]
                em.emit("vector", lambda e, h_last=h_last, cr=cr: e.tensor_copy(
                    out=cr, in_=h_last),
                    reads={"u_buf", "h_sep"}, writes={"carry"})
                cb_sl = cb.ap()[:, sh * 8 * TC:(sh + 1) * 8 * TC]
                em.emit("vector", lambda e, sh=sh, cb_sl=cb_sl, h_out=h_out: e.tensor_mul(
                    out=hc_st.ap()[:, sh * 8 * TC:(sh + 1) * 8 * TC],
                    in0=h_out.ap(), in1=cb_sl),
                    reads={"u_buf", "h_sep", "cb"}, writes={"hc_st"},
                    incs=[s_dve.bump()])
                G[("a", ap_)] = s_dve.ev()
            HT = DS * TC
            em.emit("vector", lambda e: e.tensor_add(
                out=hc_st.ap()[:, 0:HT // 2], in0=hc_st.ap()[:, 0:HT // 2],
                in1=hc_st.ap()[:, HT // 2:HT]),
                reads={"hc_st"}, writes={"hc_st"})
            em.emit("vector", lambda e: e.tensor_add(
                out=hc_st.ap()[:, 0:HT // 4], in0=hc_st.ap()[:, 0:HT // 4],
                in1=hc_st.ap()[:, HT // 4:HT // 2]),
                reads={"hc_st"}, writes={"hc_st"})
            em.emit("vector", lambda e: e.tensor_add(
                out=treeF.ap(), in0=hc_st.ap()[:, 0:2 * TC],
                in1=hc_st.ap()[:, 2 * TC:4 * TC]),
                reads={"hc_st"}, writes={"treeF"})
            em.emit("vector", lambda e: e.tensor_add(
                out=ytmp.ap(), in0=treeF.ap()[:, 0:TC], in1=treeF.ap()[:, TC:2 * TC]),
                reads={"treeF"}, writes={"ytmp"})
            em.emit("vector", lambda e, g=g, par=par: e.tensor_scalar(
                out=xcd.ap(), in0=xcs(par)[:, g * TC:(g + 1) * TC],
                scalar1=dhalf.ap()[:, g:g + 1], scalar2=None, op0=ALU.mult),
                reads={"xc"}, writes={"xcd"})
            em.emit("vector", lambda e, g=g: e.tensor_add(
                out=yb.ap()[:, g * TC:(g + 1) * TC], in0=xcd.ap(), in1=ytmp.ap()),
                reads={"xcd", "ytmp"}, writes={"yb"},
                waits=[gw("yb")] if g == 0 else (),
                incs=[s_dve.bump()])
        ev_yb = s_dve.ev()

        for g in range(NG):
            gp = g % 2
            ev_wo = dma(d_wo[gp], opw(gp), s_opT[g], waits=[gw(("opw", gp))])
            for q in range(2):
                for dmh in range(2):
                    bi = 4 + q * 2 + dmh
                    last = (q == 1 and dmh == 1)
                    mm(pb[bi].ap(),
                       yb.ap()[:, g * TC + q * 128:g * TC + q * 128 + 128],
                       opw(gp)[:, dmh * 512:(dmh + 1) * 512],
                       start=(g == 0), stop=(g == NG - 1),
                       waits=[ev_yb, ev_wo, gw(f"pb{bi}")] if g == 0 else [ev_wo],
                       incs=[s_pe.bump()] if last else (),
                       reads={"yb", "opw"}, writes={f"pb{bi}"})
            G[("opw", gp)] = s_pe.ev()
        ev_op = s_pe.ev()
        G["yb"] = ev_op

        for q in range(2):
            for dmh in range(2):
                bi = 4 + q * 2 + dmh
                em.emit("vector", lambda e, q=q, dmh=dmh, bi=bi, par=par: e.tensor_add(
                    out=r1_rot(q)[:, dmh * 512:(dmh + 1) * 512],
                    in0=pb[bi].ap(),
                    in1=x_tok(par)[:, q * 1024 + dmh * 512:q * 1024 + (dmh + 1) * 512]),
                    reads={f"pb{bi}", "x_tok"}, writes={"r1_rot"},
                    waits=[ev_op, gw(("r1", q)), gw(("r1ntok", q))],
                    incs=[s_dve.bump()])
                G[f"pb{bi}"] = s_dve.ev()
            r1q = r1_rot(q)
            em.emit("vector", lambda e, r1q=r1q: e.reduce_sum(
                out=ln_s.ap()[:, 0:1], in_=r1q, axis=AX.X),
                reads={"r1_rot"}, writes={"ln_s"})
            em.emit("scalar", lambda e, r1q=r1q: e.activation(
                out=sq_rot.ap(), in_=r1q, func=AF.Square),
                reads={"r1_rot"}, writes={"sq_rot"},
                waits=[(s_dve, s_dve.count), gw("sq")],
                incs=[s_act.bump()])
            ev_sq = s_act.ev()
            em.emit("vector", lambda e: e.reduce_sum(
                out=ln_s.ap()[:, 1:2], in_=sq_rot.ap(), axis=AX.X),
                reads={"sq_rot"}, writes={"ln_s"}, waits=[ev_sq])
            em.emit("vector", lambda e: e.tensor_scalar(
                out=ln_s.ap()[:, 2:3], in0=ln_s.ap()[:, 0:1], scalar1=1.0 / D,
                scalar2=None, op0=ALU.mult),
                reads={"ln_s"}, writes={"ln_s"})
            em.emit("vector", lambda e: e.tensor_mul(
                out=ln_s.ap()[:, 3:4], in0=ln_s.ap()[:, 2:3], in1=ln_s.ap()[:, 2:3]),
                reads={"ln_s"}, writes={"ln_s"})
            em.emit("vector", lambda e: e.tensor_scalar(
                out=ln_s.ap()[:, 4:5], in0=ln_s.ap()[:, 1:2], scalar1=1.0 / D,
                scalar2=None, op0=ALU.mult),
                reads={"ln_s"}, writes={"ln_s"})
            em.emit("vector", lambda e: e.tensor_sub(
                out=ln_s.ap()[:, 5:6], in0=ln_s.ap()[:, 4:5], in1=ln_s.ap()[:, 3:4]),
                reads={"ln_s"}, writes={"ln_s"},
                incs=[s_dve.bump()])
            ev_var = s_dve.ev()
            G["sq"] = ev_var
            em.emit("scalar", lambda e: e.activation(
                out=ln_s.ap()[:, 6:7], in_=ln_s.ap()[:, 5:6], func=AF.Ln,
                bias=cst.ap()[:, 0:1]),
                reads={"ln_s"}, writes={"ln_s"}, waits=[ev_var])
            em.emit("scalar", lambda e: e.activation(
                out=ln_s.ap()[:, 7:8], in_=ln_s.ap()[:, 6:7], func=AF.Exp,
                scale=cst.ap()[:, 1:2]),
                reads={"ln_s"}, writes={"ln_s"},
                incs=[s_act.bump()])
            ev_rstd = s_act.ev()
            em.emit("vector", lambda e, q=q, r1q=r1q: e.tensor_scalar(
                out=r1q, in0=r1q, scalar1=ln_s.ap()[:, 2:3],
                scalar2=ln_s.ap()[:, 7:8], op0=ALU.subtract, op1=ALU.mult),
                reads={"r1_rot", "ln_s"}, writes={"r1_rot"},
                waits=[ev_rstd],
                incs=[s_dve.bump()])
            ev_r1n = s_dve.ev()
            t0 = c * TC + q * 128
            G[("r1ntok", q)] = dma(d_rtw[q], s_r1n[t0:t0 + 128, :], r1q,
                                   waits=[ev_r1n], eng=GPQ)
            for half in range(2):
                bsel = 4 + half
                for kd in range(4):
                    kdg = half * 4 + kd
                    em.emit("tensor", lambda e, q=q, kdg=kdg, kd=kd, bsel=bsel, r1q=r1q: e.transpose(
                        out=pb[bsel].ap()[:, kd * 128:(kd + 1) * 128],
                        in_=r1q[:, kdg * 128:(kdg + 1) * 128],
                        identity=ident_f.ap()),
                        reads={"r1_rot", "ident_f"}, writes={f"pb{bsel}"},
                        waits=[ev_r1n, gw(f"pb{bsel}")] if kd == 0 else (),
                        incs=[s_pe.bump()] if kd == 3 else ())
                ev_tr2 = s_pe.ev()
                em.emit("scalar", lambda e, q=q, half=half, bsel=bsel: e.copy(
                    out=r1nT_rot(q)[:, half * 512:(half + 1) * 512],
                    in_=pb[bsel].ap()),
                    reads={f"pb{bsel}"}, writes={"r1nT_rot"},
                    waits=[ev_tr2, gw(("r1nT", q)) if half == 0 else None])
                em.emit("scalar", lambda e, q=q, half=half, bsel=bsel: e.copy(
                    out=gate_rot(q)[:, half * 512:(half + 1) * 512],
                    in_=pb[bsel].ap()),
                    reads={f"pb{bsel}"}, writes={"gate_rot"},
                    waits=[gw(("gate", q)) if half == 0 else None],
                    incs=[s_act.bump()])
                G[f"pb{bsel}"] = s_act.ev()
            ev_gtr = s_act.ev()
            G[("r1", q)] = ev_gtr
            for kd in range(KD):
                dma(d_rTw[q], s_r1nT[kd][:, t0:t0 + 128],
                    r1nT_rot(q)[:, kd * 128:(kd + 1) * 128],
                    waits=[ev_gtr] if kd == 0 else (), eng=GPQ)
            G[("r1nT", q)] = d_rTw[q].ev()
            for kd in range(KD):
                mm(pb[4].ap()[:, 0:8],
                   gate_rot(q)[:, kd * 128:(kd + 1) * 128],
                   wgr.ap()[:, kd * 8:(kd + 1) * 8],
                   start=(kd == 0), stop=(kd == KD - 1),
                   waits=[ev_gtr, gw("pb4")] if kd == 0 else (),
                   incs=[s_pe.bump()] if kd == KD - 1 else (),
                   reads={"gate_rot", "wgr"}, writes={"pb4"})
            ev_gmm = s_pe.ev()
            G[("gate", q)] = ev_gmm
            em.emit("scalar", lambda e: e.copy(
                out=lgs.ap()[:, 0:8], in_=pb[4].ap()[:, 0:8]),
                reads={"pb4"}, writes={"lgs"},
                waits=[ev_gmm, gw("lgs")],
                incs=[s_act.bump()])
            ev_lg = s_act.ev()
            G["pb4"] = ev_lg
            L = lgs.ap()
            em.emit("vector", lambda e, L=L: e.reduce_max(
                out=L[:, 8:9], in_=L[:, 0:8], axis=AX.X),
                reads={"lgs"}, writes={"lgs"}, waits=[ev_lg])
            em.emit("vector", lambda e, L=L: e.tensor_scalar(
                out=L[:, 16:24], in0=L[:, 0:8], scalar1=L[:, 8:9], scalar2=None,
                op0=ALU.is_equal),
                reads={"lgs"}, writes={"lgs"})
            em.emit("vector", lambda e, L=L: e.scalar_tensor_tensor(
                out=L[:, 24:32], in0=L[:, 16:24], scalar=-1e5, in1=L[:, 0:8],
                op0=ALU.mult, op1=ALU.add),
                reads={"lgs"}, writes={"lgs"})
            em.emit("vector", lambda e, L=L: e.reduce_max(
                out=L[:, 9:10], in_=L[:, 24:32], axis=AX.X),
                reads={"lgs"}, writes={"lgs"})
            em.emit("vector", lambda e, L=L: e.tensor_sub(
                out=L[:, 10:11], in0=L[:, 9:10], in1=L[:, 8:9]),
                reads={"lgs"}, writes={"lgs"},
                incs=[s_dve.bump()])
            ev_d = s_dve.ev()
            em.emit("scalar", lambda e, L=L: e.activation(
                out=L[:, 11:12], in_=L[:, 10:11], func=AF.Exp),
                reads={"lgs"}, writes={"lgs"}, waits=[ev_d],
                incs=[s_act.bump()])
            ev_e = s_act.ev()
            em.emit("vector", lambda e, L=L: e.tensor_scalar(
                out=L[:, 12:13], in0=L[:, 11:12], scalar1=1.0, scalar2=None,
                op0=ALU.add),
                reads={"lgs"}, writes={"lgs"}, waits=[ev_e])
            em.emit("vector", lambda e, L=L: e.reciprocal(
                out=L[:, 13:14], in_=L[:, 12:13]),
                reads={"lgs"}, writes={"lgs"})
            em.emit("vector", lambda e, L=L: e.tensor_mul(
                out=L[:, 14:15], in0=L[:, 11:12], in1=L[:, 13:14]),
                reads={"lgs"}, writes={"lgs"})
            em.emit("vector", lambda e, L=L: e.tensor_scalar(
                out=L[:, 32:40], in0=L[:, 24:32], scalar1=L[:, 9:10], scalar2=None,
                op0=ALU.is_equal),
                reads={"lgs"}, writes={"lgs"})
            em.emit("vector", lambda e, L=L: e.tensor_scalar(
                out=L[:, 40:48], in0=L[:, 16:24], scalar1=L[:, 13:14], scalar2=None,
                op0=ALU.mult),
                reads={"lgs"}, writes={"lgs"})
            tt = c * 2 + q
            em.emit("vector", lambda e, L=L, tt=tt: e.scalar_tensor_tensor(
                out=combine.ap()[:, tt * 8:(tt + 1) * 8], in0=L[:, 32:40],
                scalar=L[:, 14:15], in1=L[:, 40:48], op0=ALU.mult, op1=ALU.add),
                reads={"lgs"}, writes={"combine"},
                incs=[s_dve.bump()])
            G["lgs"] = s_dve.ev()
        G[("x_tok", par)] = s_dve.ev()
        G["bbcb"] = ev_yb

    if KSKEW:
        for c in range(KNCH):
            mm_phase(c)
            if c >= 1:
                scan_phase(c - 1)
        if KNCH:
            scan_phase(KNCH - 1)
    else:
        for c in range(KNCH):
            mm_phase(c)
            scan_phase(c)

    a_pe = s_pe.ev()
    a_act = s_act.ev()
    a_dve = s_dve.ev()
    a_rtw = [d_rtw[0].ev(), d_rtw[1].ev()]
    a_rTw = [d_rTw[0].ev(), d_rTw[1].ev()]

    for g_ in reversed(ga):
        g_.__exit__(None, None, None)

    r1nT_sl = nc.alloc_sbuf_tensor("r1nT_sl", [128, KD * SLAB], BF16)
    w1w_t = nc.alloc_sbuf_tensor("w1w", [128, 2 * 1024], BF16)
    w1w = lambda i: w1w_t.ap()[:, i * 1024:(i + 1) * 1024]
    w2w_t = nc.alloc_sbuf_tensor("w2w", [128, 2 * 1024], BF16)
    w2w = lambda i: w2w_t.ap()[:, i * 1024:(i + 1) * 1024]
    w3w_t = nc.alloc_sbuf_tensor("w3w", [128, 2 * 1024], BF16)
    w3w = lambda i: w3w_t.ap()[:, i * 1024:(i + 1) * 1024]
    pprime = nc.alloc_sbuf_tensor("pprime", [128, NHK * SLAB], BF16)
    bs0_t = nc.alloc_sbuf_tensor("bs0", [128, 2 * SLAB], F32)
    bs0 = lambda i: bs0_t.ap()[:, i * SLAB:(i + 1) * SLAB]
    bsA_t = nc.alloc_sbuf_tensor("bsA", [128, 2 * SLAB], F32)
    bsA = lambda i: bsA_t.ap()[:, i * SLAB:(i + 1) * SLAB]
    sH_t = nc.alloc_sbuf_tensor("sHB", [128, 2 * SLAB], F32)
    sH = lambda i: sH_t.ap()[:, i * SLAB:(i + 1) * SLAB]
    accB = nc.alloc_sbuf_tensor("accB", [128, 4 * 1024], F32)
    r1tok_t = nc.alloc_sbuf_tensor("r1tok", [128, 2 * 1024], F32)
    r1tok = lambda i: r1tok_t.ap()[:, i * 1024:(i + 1) * 1024]
    outr_t = nc.alloc_sbuf_tensor("outr", [128, 2 * 1024], F32)
    outr = lambda i: outr_t.ap()[:, i * 1024:(i + 1) * 1024]
    sqB = nc.alloc_sbuf_tensor("sqB", [128, 1024], F32)
    lnB = nc.alloc_sbuf_tensor("lnB", [128, 16], F32)

    barrier = [a_pe, a_act, a_dve, a_rtw[0], a_rtw[1], a_rTw[0], a_rTw[1],
               moe_prep_done[0], moe_prep_done[1]]

    for sl in range(min(NSL, KNSL)):
        t0 = sl * SLAB
        sp = sl % 2
        ev_r1t = dma(d_rr[sp],
                     r1nT_sl.ap().rearrange("p (kd t) -> p kd t", kd=KD),
                     s_r1nT.rearrange("kd p t -> p kd t")[:, :, t0:t0 + SLAB],
                     waits=(barrier if sl == 0 else [gw("r1nT_sl")]))
        for e_i in range(NE):
            w12ev = {}

            def ld12(hg, e_i=e_i, w12ev=w12ev):
                wp = hg % 2
                dma(d_wa[wp], w1w(wp), s_w1[e_i, hg], waits=[gw(("w1w", wp))])
                ev1 = d_wa[wp].ev()
                dma(d_wo[wp], w2w(wp), s_w2[e_i, hg], waits=[gw(("w2w", wp))])
                w12ev[hg] = (ev1, d_wo[wp].ev())
            ld12(0)
            ld12(1)
            for hg in range(NHK):
                wp = hg % 2
                sp_ = hg % 2
                ev_w1, ev_w2 = w12ev[hg]
                bH = hg % 2
                bG = 2 + hg % 2
                for k in range(KD):
                    mm(pb[bH].ap(),
                       w1w(wp)[:, k * 128:(k + 1) * 128],
                       r1nT_sl.ap()[:, k * SLAB:(k + 1) * SLAB],
                       start=(k == 0), stop=(k == KD - 1),
                       waits=[ev_w1, ev_r1t, gw(f"pb{bH}"),
                              (barrier[0] if (sl == 0 and e_i == 0 and hg == 0) else None)] if k == 0 else (),
                       incs=[s_pe.bump()] if k == KD - 1 else (),
                       reads={"w1w", "r1nT_sl"}, writes={f"pb{bH}"})
                ev_H = s_pe.ev()
                G[("w1w", wp)] = ev_H
                for k in range(KD):
                    mm(pb[bG].ap(),
                       w2w(wp)[:, k * 128:(k + 1) * 128],
                       r1nT_sl.ap()[:, k * SLAB:(k + 1) * SLAB],
                       start=(k == 0), stop=(k == KD - 1),
                       waits=[ev_w2, gw(f"pb{bG}")] if k == 0 else (),
                       incs=[s_pe.bump()] if k == KD - 1 else (),
                       reads={"w2w", "r1nT_sl"}, writes={f"pb{bG}"})
                ev_G = s_pe.ev()
                G[("w2w", wp)] = ev_G
                if hg + 2 < NHK:
                    ld12(hg + 2)
                if KINPACT:
                    em.emit("scalar", lambda e, bH=bH, sp_=sp_: e.activation(
                        out=bs0(sp_), in_=pb[bH].ap(), func=AF.Exp, scale=-1.0),
                        reads={f"pb{bH}"}, writes={f"bs0{sp_}"},
                        waits=[ev_H, gw(("bsg", sp_))])
                    em.emit("scalar", lambda e, sp_=sp_: e.activation(
                        out=bs0(sp_), in_=bs0(sp_), func=AF.Ln, bias=1.0),
                        reads={f"bs0{sp_}"}, writes={f"bs0{sp_}"})
                    em.emit("scalar", lambda e, sp_=sp_: e.activation(
                        out=bs0(sp_), in_=bs0(sp_), func=AF.Exp, scale=-1.0),
                        reads={f"bs0{sp_}"}, writes={f"bs0{sp_}"},
                        incs=[s_act.bump()])
                else:
                    em.emit("scalar", lambda e, bH=bH, sp_=sp_: e.activation(
                        out=bsA(sp_), in_=pb[bH].ap(), func=AF.Exp, scale=-1.0),
                        reads={f"pb{bH}"}, writes={f"bsA{sp_}"},
                        waits=[ev_H, gw(("bsg", sp_))])
                    em.emit("scalar", lambda e, sp_=sp_: e.activation(
                        out=bs0(sp_), in_=bsA(sp_), func=AF.Ln, bias=1.0),
                        reads={f"bsA{sp_}"}, writes={f"bs0{sp_}"})
                    em.emit("scalar", lambda e, sp_=sp_: e.activation(
                        out=bs0(sp_), in_=bs0(sp_), func=AF.Exp, scale=-1.0),
                        reads={f"bs0{sp_}"}, writes={f"bs0{sp_}"},
                        incs=[s_act.bump()])
                ev_sg = s_act.ev()
                em.emit("vector", lambda e, bH=bH, sp_=sp_: e.tensor_mul(
                    out=sH(sp_), in0=bs0(sp_), in1=pb[bH].ap()),
                    reads={f"bs0{sp_}", f"pb{bH}"}, writes={f"sHB{sp_}"},
                    waits=[ev_sg],
                    incs=[s_dve.bump()])
                G[f"pb{bH}"] = s_dve.ev()
                G[("bsg", sp_)] = s_dve.ev()
                em.emit("vector", lambda e, hg=hg, bG=bG, sp_=sp_: e.tensor_mul(
                    out=pprime.ap()[:, hg * SLAB:(hg + 1) * SLAB],
                    in0=sH(sp_), in1=pb[bG].ap()),
                    reads={f"sHB{sp_}", f"pb{bG}"}, writes={"pprime"},
                    waits=[ev_G, gw("pprime") if hg == 0 else None],
                    incs=[s_dve.bump()])
                G[f"pb{bG}"] = s_dve.ev()
            ev_pp = s_dve.ev()
            for dmh in range(2):
                w3ev = {}

                def ld3(hk, e_i=e_i, w3ev=w3ev):
                    wp3 = hk % 2
                    dma(d_wb[wp3], w3w(wp3), s_w3[e_i, hk],
                        waits=[gw(("w3w", wp3))])
                    w3ev[hk] = d_wb[wp3].ev()
                ld3(0)
                ld3(1)
                for hk in range(NHK):
                    wp3 = hk % 2
                    for tt in range(4):
                        mm(pb[tt].ap(),
                           pprime.ap()[:, hk * SLAB + tt * 128:hk * SLAB + tt * 128 + 128],
                           w3w(wp3)[:, dmh * 512:(dmh + 1) * 512],
                           start=(hk == 0), stop=(hk == NHK - 1),
                           waits=([w3ev[hk], ev_pp, gw(f"pb{tt}")]
                                  if hk == 0 else
                                  ([w3ev[hk]] if tt == 0 else ())),
                           incs=[s_pe.bump()] if tt == 3 else (),
                           reads={"pprime", "w3w"}, writes={f"pb{tt}"})
                    G[("w3w", wp3)] = s_pe.ev()
                    if hk + 2 < NHK:
                        ld3(hk + 2)
                for tt in range(4):
                    ev_Y = s_pe.ev()
                    tt_g = sl * 4 + tt
                    ce = combine.ap()[:, tt_g * NE + e_i:tt_g * NE + e_i + 1]
                    acc_sl = accB.ap()[:, tt * 1024 + dmh * 512:tt * 1024 + (dmh + 1) * 512]
                    if e_i == 0:
                        em.emit("vector", lambda e, acc_sl=acc_sl, ce=ce, tt=tt: e.tensor_scalar(
                            out=acc_sl, in0=pb[tt].ap(), scalar1=ce, scalar2=None,
                            op0=ALU.mult),
                            reads={f"pb{tt}", "combine"}, writes={"accB"},
                            waits=[ev_Y, gw("accB") if (tt == 0 and dmh == 0) else None],
                            incs=[s_dve.bump()])
                    else:
                        em.emit("vector", lambda e, acc_sl=acc_sl, ce=ce, tt=tt: e.scalar_tensor_tensor(
                            out=acc_sl, in0=pb[tt].ap(), scalar=ce, in1=acc_sl,
                            op0=ALU.mult, op1=ALU.add),
                            reads={f"pb{tt}", "combine", "accB"}, writes={"accB"},
                            waits=[ev_Y],
                            incs=[s_dve.bump()])
                    G[f"pb{tt}"] = s_dve.ev()
            G["pprime"] = s_pe.ev()
        G["r1nT_sl"] = s_pe.ev()
        for tt in range(4):
            tp = tt % 2
            tg0 = t0 + tt * 128
            ev_r1 = dma(d_rt[tp], r1tok(tp), s_r1n[tg0:tg0 + 128, :],
                        waits=[gw(("r1tok", tp))])
            em.emit("vector", lambda e, tp=tp, tt=tt: e.tensor_add(
                out=outr(tp), in0=accB.ap()[:, tt * 1024:(tt + 1) * 1024],
                in1=r1tok(tp)),
                reads={"accB", "r1tok"}, writes={"outr"},
                waits=[ev_r1, gw(("outr", tp))],
                incs=[s_dve.bump()])
            G[("r1tok", tp)] = s_dve.ev()
            r2q = outr(tp)
            em.emit("vector", lambda e, r2q=r2q: e.reduce_sum(
                out=lnB.ap()[:, 0:1], in_=r2q, axis=AX.X),
                reads={"outr"}, writes={"lnB"})
            em.emit("scalar", lambda e, r2q=r2q: e.activation(
                out=sqB.ap(), in_=r2q, func=AF.Square),
                reads={"outr"}, writes={"sqB"},
                waits=[(s_dve, s_dve.count), gw("sqB")],
                incs=[s_act.bump()])
            ev_sqB = s_act.ev()
            em.emit("vector", lambda e: e.reduce_sum(
                out=lnB.ap()[:, 1:2], in_=sqB.ap(), axis=AX.X),
                reads={"sqB"}, writes={"lnB"}, waits=[ev_sqB])
            em.emit("vector", lambda e: e.tensor_scalar(
                out=lnB.ap()[:, 2:3], in0=lnB.ap()[:, 0:1], scalar1=1.0 / D,
                scalar2=None, op0=ALU.mult),
                reads={"lnB"}, writes={"lnB"})
            em.emit("vector", lambda e: e.tensor_mul(
                out=lnB.ap()[:, 3:4], in0=lnB.ap()[:, 2:3], in1=lnB.ap()[:, 2:3]),
                reads={"lnB"}, writes={"lnB"})
            em.emit("vector", lambda e: e.tensor_scalar(
                out=lnB.ap()[:, 4:5], in0=lnB.ap()[:, 1:2], scalar1=1.0 / D,
                scalar2=None, op0=ALU.mult),
                reads={"lnB"}, writes={"lnB"})
            em.emit("vector", lambda e: e.tensor_sub(
                out=lnB.ap()[:, 5:6], in0=lnB.ap()[:, 4:5], in1=lnB.ap()[:, 3:4]),
                reads={"lnB"}, writes={"lnB"},
                incs=[s_dve.bump()])
            ev_v = s_dve.ev()
            G["sqB"] = ev_v
            em.emit("scalar", lambda e: e.activation(
                out=lnB.ap()[:, 6:7], in_=lnB.ap()[:, 5:6], func=AF.Ln,
                bias=cst.ap()[:, 0:1]),
                reads={"lnB"}, writes={"lnB"}, waits=[ev_v])
            em.emit("scalar", lambda e: e.activation(
                out=lnB.ap()[:, 7:8], in_=lnB.ap()[:, 6:7], func=AF.Exp,
                scale=cst.ap()[:, 1:2]),
                reads={"lnB"}, writes={"lnB"},
                incs=[s_act.bump()])
            ev_rs = s_act.ev()
            em.emit("vector", lambda e, tp=tp, r2q=r2q: e.tensor_scalar(
                out=outr(tp), in0=r2q, scalar1=lnB.ap()[:, 2:3],
                scalar2=lnB.ap()[:, 7:8], op0=ALU.subtract, op1=ALU.mult),
                reads={"outr", "lnB"}, writes={"outr"}, waits=[ev_rs],
                incs=[s_dve.bump()])
            ev_o = s_dve.ev()
            G[("outr", tp)] = dma(d_out[tp], out_dr[tg0:tg0 + 128, :], outr(tp),
                                  waits=[ev_o], eng=GPQ)
        G["accB"] = s_dve.ev()

    fin = [d_out[0].ev(), d_out[1].ev(), d_ms.ev(), d_msb[0].ev(), d_msb[1].ev(),
           d_rtw[0].ev(), d_rtw[1].ev(), d_rTw[0].ev(), d_rTw[1].ev()]
    em.emit("sync", lambda e: e.nop(), waits=fin)

    with nc.Block() as block:
        em.replay(block)
    return nc


def kernel(**inputs):
    x = np.ascontiguousarray(np.asarray(inputs["L0_attn_x"], dtype=np.float32))
    conv_w = np.ascontiguousarray(
        np.asarray(inputs["L0_attn_conv_w"], dtype=np.float32).reshape(DI, 4))
    common = {
        "x_proj_w": np.ascontiguousarray(np.asarray(inputs["L0_attn_x_proj_w"], np.float32)),
        "dt_proj_w": np.ascontiguousarray(np.asarray(inputs["L0_attn_dt_proj_w"], np.float32)),
        "dt_proj_b": np.ascontiguousarray(np.asarray(inputs["L0_attn_dt_proj_b"], np.float32)),
        "A_log": np.ascontiguousarray(np.asarray(inputs["L0_attn_A_log"], np.float32)),
        "Dvec": np.ascontiguousarray(np.asarray(inputs["L0_attn_D"], np.float32)),
        "B_proj_w": np.ascontiguousarray(np.asarray(inputs["L0_attn_B_proj_w"], np.float32)),
        "C_proj_w": np.ascontiguousarray(np.asarray(inputs["L0_attn_C_proj_w"], np.float32)),
        "conv_w": conv_w,
        "conv_b": np.ascontiguousarray(np.asarray(inputs["L0_attn_conv_b"], np.float32)),
        "out_proj_w": np.ascontiguousarray(np.asarray(inputs["L0_attn_out_proj_w"], np.float32)),
        "Wg": np.ascontiguousarray(np.asarray(inputs["L0_ffn_Wg"], np.float32)),
        "W1": np.ascontiguousarray(np.asarray(inputs["L0_ffn_W1"], np.float32)),
        "W2": np.ascontiguousarray(np.asarray(inputs["L0_ffn_W2"], np.float32)),
        "W3": np.ascontiguousarray(np.asarray(inputs["L0_ffn_W3"], np.float32)),
    }
    in_maps = []
    for i in range(8):
        m = {"x": np.ascontiguousarray(x[i])}
        m.update(common)
        in_maps.append(m)
    nc = build_program()
    res = run_bass_kernel_spmd(nc, in_maps, core_ids=list(range(8)))
    return np.stack([np.asarray(r["out"]) for r in res.results], axis=0)


if __name__ == "__main__":
    nc = build_program()
    print("built:", nc.n_instructions())
